# version 11 (final)
# speedup vs baseline: 1.4020x; 1.0252x over previous
import numpy as np
import ml_dtypes
import concourse.bass as bass
import concourse.mybir as mybir
from concourse.bass_utils import run_bass_kernel_spmd

F32 = mybir.dt.float32
BF16 = mybir.dt.bfloat16
I32 = mybir.dt.int32
AF = mybir.ActivationFunctionType
ALU = mybir.AluOpType
AX = mybir.AxisListType

D = 1024
HF = 4096
NT = 16
ROWS = 2048
NKC = 8
WZ = 2304
ENGS = ("sync", "vector", "scalar", "tensor", "gpsimd")
bfnp = ml_dtypes.bfloat16


class Sem:
    def __init__(self, handle):
        self.h = handle
        self.n = 0


class Prog:
    def __init__(self):
        self.nc = bass.Bass("TRN2", target_bir_lowering=False)
        self.q = {e: [] for e in ENGS}
        self.bar = Sem(self.nc.alloc_semaphore("bar"))
        self._names = {}
        self.eng_sems = {e: [] for e in ENGS}

    def _nm(self, base):
        k = self._names.get(base, 0)
        self._names[base] = k + 1
        return f"{base}_{k}" if k else base

    def sbuf(self, name, shape, dt):
        return self.nc.alloc_sbuf_tensor(self._nm(name), shape, dt).ap()

    def psum(self, name, shape, dt=F32):
        return self.nc.alloc_psum_tensor(self._nm(name), shape, dt).ap()

    def dram(self, name, shape, dt, **kw):
        return self.nc.dram_tensor(self._nm(name), shape, dt, kind="Internal", **kw).ap()

    def sem(self, name="s"):
        return Sem(self.nc.alloc_semaphore(self._nm(name)))

    def const(self, arr, name="c"):
        return self.nc.inline_tensor(np.ascontiguousarray(arr), name=self._nm(name)).ap()

    def post(self, eng, fn, *incs):
        self.q[eng].append(fn)
        for s, k in incs:
            s.n += k

    def wait(self, eng, sem, val=None):
        v = sem.n if val is None else val
        self.q[eng].append(lambda e, v=v, h=sem.h: e.wait_ge(h, v))

    def chain(self, eng, seq, nops, builder):
        base = seq.n
        def f(e, base=base):
            cnt = [base]
            def S(instr):
                instr.then_inc(seq.h, 1)
                cnt[0] += 1
                return instr
            def W():
                e.wait_ge(seq.h, cnt[0])
            builder(e, S, W)
            assert cnt[0] - base == nops, f"chain {eng}: {cnt[0]-base} != {nops}"
        self.post(eng, f, (seq, nops))

    def barrier(self):
        b = self.bar
        target = b.n + len(ENGS)
        for e in ENGS:
            waits = [(s.h, s.n) for s in self.eng_sems[e]]
            def f(eng, waits=waits, target=target):
                for h, v in waits:
                    eng.wait_ge(h, v)
                eng.nop().then_inc(b.h, 1)
                eng.wait_ge(b.h, target)
            self.q[e].append(f)
        b.n = target

    def finalize(self):
        nc = self.nc
        with nc.Block() as block:
            @block.sync
            def _(e):
                for fn in self.q["sync"]:
                    fn(e)

            @block.vector
            def _(e):
                for fn in self.q["vector"]:
                    fn(e)

            @block.scalar
            def _(e):
                for fn in self.q["scalar"]:
                    fn(e)

            @block.tensor
            def _(e):
                for fn in self.q["tensor"]:
                    fn(e)

            @block.gpsimd
            def _(e):
                for fn in self.q["gpsimd"]:
                    fn(e)
        return nc



def build_launch1(upto=4):
    p = Prog()
    nc = p.nc
    post, wait, barrier, chain = p.post, p.wait, p.barrier, p.chain

    x_in = nc.dram_tensor("x0", [ROWS, D], F32, kind="ExternalInput").ap()
    xhalo_in = nc.dram_tensor("xhalo", [256, D], F32, kind="ExternalInput").ap()
    x_out = nc.dram_tensor("xout", [ROWS, D], F32, kind="ExternalOutput").ap()
    poscols_in = nc.dram_tensor("poscols", [128, NT + 2], F32, kind="ExternalInput").ap()
    w_aa = {w: nc.dram_tensor(f"aa_{w}", [D, D], F32, kind="ExternalInput").ap()
            for w in ("Wr", "Wk", "Wv", "Wo")}
    aa_decay_t = nc.dram_tensor("aa_decay_t", [128, D], F32, kind="ExternalInput").ap()
    aa_lnw_t = nc.dram_tensor("aa_lnw_t", [128, D], F32, kind="ExternalInput").ap()
    ffn_w1T = nc.dram_tensor("ffn_w1T", [D, HF], F32, kind="ExternalInput").ap()
    ffn_w2T = nc.dram_tensor("ffn_w2T", [HF, D], F32, kind="ExternalInput").ap()
    ffn_g1_t = nc.dram_tensor("ffn_g1_t", [128, D], F32, kind="ExternalInput").ap()
    ffn_b1ln_t = nc.dram_tensor("ffn_b1ln_t", [128, D], F32, kind="ExternalInput").ap()
    ffn_g2_t = nc.dram_tensor("ffn_g2_t", [128, HF], F32, kind="ExternalInput").ap()
    ffn_b2ln_t = nc.dram_tensor("ffn_b2ln_t", [128, HF], F32, kind="ExternalInput").ap()
    ffn_bias1_t = nc.dram_tensor("ffn_bias1_t", [128, HF], F32, kind="ExternalInput").ap()
    ffn_bias2_t = nc.dram_tensor("ffn_bias2_t", [128, D], F32, kind="ExternalInput").ap()
    ab_w = {w: nc.dram_tensor(f"ab_{w}T", [D, D], F32, kind="ExternalInput").ap()
            for w in ("Wq", "Wk", "Wv", "Wo")}
    ab_bq_c = nc.dram_tensor("ab_bq_c", [128, 8], F32, kind="ExternalInput").ap()
    ab_bk_c = nc.dram_tensor("ab_bk_c", [128, 8], F32, kind="ExternalInput").ap()
    ab_bv_t = nc.dram_tensor("ab_bv_t", [128, D], F32, kind="ExternalInput").ap()
    ab_bo_t = nc.dram_tensor("ab_bo_t", [128, D], F32, kind="ExternalInput").ap()
    ab_mb0 = nc.dram_tensor("ab_mb0", [64, 128], F32, kind="ExternalInput").ap()
    ab_mg0 = nc.dram_tensor("ab_mg0", [1, 128], F32, kind="ExternalInput").ap()

    ltri_np = (np.arange(128)[:, None] <= np.arange(128)[None, :]).astype(np.float32)
    c_ltri = p.const(ltri_np, "ltri")
    dqk = np.arange(128)[None, :] - np.arange(128)[:, None]
    c_maskA = p.const(((dqk >= 0) & (dqk <= 64)).astype(np.float32), "maskA")
    c_maskT = p.const((np.arange(128)[None, :] <= np.arange(64)[:, None])
                      .astype(np.float32), "maskT")

    xs = p.sbuf("xs", [128, NT * D], F32)
    h_fm = p.sbuf("h_fm", [128, NKC * ROWS], BF16)
    AR_W = p.sbuf("AR_W", [128, 32 * D], BF16)
    wk_t = AR_W[:, 0:NKC * D]
    wv_t = AR_W[:, NKC * D:2 * NKC * D]
    wr_t = AR_W[:, 2 * NKC * D:3 * NKC * D]
    w1q_t = AR_W[:, 0:32 * D]
    w2q_t = AR_W[:, 0:32 * D]
    AR_AUX = p.sbuf("AR_AUX", [128, 15 * D], BF16)
    sck_tile = AR_AUX[:, 0:D]
    scv_tile = AR_AUX[:, D:2 * D]
    scr_tile = AR_AUX[:, 2 * D:3 * D]
    sco_tile = AR_AUX[:, 3 * D:4 * D]
    lnw_tile = AR_AUX[:, 4 * D:5 * D]
    ld_tile = AR_AUX[:, 6 * D:8 * D].bitcast(F32)
    g1_tile = AR_AUX[:, 0:D]
    b1ln_tile = AR_AUX[:, D:2 * D]
    bias2_tile = AR_AUX[:, 2 * D:3 * D]
    g2_tile = AR_AUX[:, 3 * D:7 * D]
    b2ln_tile = AR_AUX[:, 7 * D:11 * D]
    bias1_tile = AR_AUX[:, 11 * D:15 * D]
    bv_tile_b = AR_AUX[:, 0:D]
    bo_tile_b = AR_AUX[:, D:2 * D]
    kt2_sb = AR_AUX[0:64, 2 * D:6 * D]
    qt_sb = AR_AUX[0:64, 6 * D:8 * D]
    v2_sb = AR_AUX[:, 8 * D:10 * D]
    vp64_sb = AR_AUX[0:64, 10 * D:12 * D]
    kstg = AR_AUX[:, 12 * D:13 * D]
    vg_row = AR_AUX[0:1, 13 * D:14 * D]
    maskA_b = AR_AUX[:, 14 * D:14 * D + 128]
    maskT_b = AR_AUX[0:64, 14 * D + 128:14 * D + 256]
    mb0_b = AR_AUX[0:64, 14 * D + 256:14 * D + 384]
    mg0_b = AR_AUX[0:1, 14 * D + 384:14 * D + 512]
    pa_sb2 = [AR_AUX[:, 14 * D + 512:14 * D + 640], AR_AUX[:, 12 * D:12 * D + 128]]
    pbg_sb2 = [AR_AUX[0:64, 14 * D + 640:14 * D + 768],
               AR_AUX[0:64, 12 * D + 128:12 * D + 256]]
    pg_sb2 = [AR_AUX[0:1, 14 * D + 768:14 * D + 896],
              AR_AUX[0:1, 12 * D + 256:12 * D + 384]]
    ktg_b = AR_AUX[0:64, 14 * D + 896:14 * D + 912]
    ktg8 = AR_AUX[:, 14 * D + 912:14 * D + 920]
    onescb = AR_AUX[:, 14 * D + 920:14 * D + 921]
    wf32 = h_fm[:, 0:2 * D].bitcast(F32)
    wf32b = h_fm[:, 2 * D:4 * D].bitcast(F32)
    scl_cols = p.sbuf("scl_cols", [128, NKC], F32)
    carry = p.sbuf("carry", [1, D], F32)
    rowtmp = carry[0:1, 0:128]
    pos_sb = p.sbuf("pos_sb", [128, NT + 2], F32)
    stat = p.sbuf("stat", [128, 72], F32)
    onesr = p.sbuf("onesr", [1, 128], F32)
    onesc = p.sbuf("onesc", [128, 1], F32)
    ltri_sb = p.sbuf("ltri_sb", [128, 128], F32)
    t_a = p.sbuf("t_a", [128, D], F32)
    t_b = p.sbuf("t_b", [128, D], F32)
    t_c = p.sbuf("t_c", [128, D], F32)
    q32 = t_b.bitcast(I32)

    pA = p.psum("pA", [128, 2048])
    pB = p.psum("pB", [128, 2048])
    pk = pA[:, 0:1024]
    pv = pA[:, 1024:2048]
    pr = pB[:, 0:1024]
    pc = pB[:, 1024:2048]

    hscr = p.dram("hscr", [WZ, D], BF16)
    x1halo_d = p.dram("x1halo_d", [256, D], F32)
    zscr = p.dram("zscr", [WZ, D], BF16)
    wscr = p.dram("wscr", [D, D], BF16)
    rowscr8 = p.dram("rowscr8", [1, D], F32)
    q2scr = p.dram("q2scr", [WZ, HF], BF16)
    w1qscr = p.dram("w1qscr", [D, HF], BF16)
    w2qscr = p.dram("w2qscr", [HF, D], BF16)

    s_ld = p.sem("s_ld")
    s_w = p.sem("s_w")
    s_w2 = p.sem("s_w2")
    s_tr = p.sem("s_tr")
    s_mm = p.sem("s_mm")
    s_ve = p.sem("s_ve")
    s_ac = p.sem("s_ac")
    s_gp = p.sem("s_gp")
    s_hb = p.sem("s_hb")
    p.eng_sems["sync"] = [s_ld, s_w, s_w2, s_tr]
    p.eng_sems["vector"] = [s_ve]
    p.eng_sems["scalar"] = [s_ac]
    p.eng_sems["tensor"] = [s_mm]
    p.eng_sems["gpsimd"] = [s_gp, s_hb]

    TOFF = lambda t: (t if t < NT else t - NT) * 128

    def dma(eng, out, in_, sem, n=16):
        def f(e, out=out, in_=in_):
            e.dma_start(out=out, in_=in_).then_inc(sem.h, n)
        post(eng, f, (sem, n))

    dma("sync", xs[:].rearrange("p (t d) -> p t d", d=D),
        x_in.rearrange("(t p) d -> p t d", p=128), s_ld)
    dma("sync", pos_sb[:], poscols_in, s_ld)
    dma("sync", ld_tile[:], aa_decay_t, s_ld)
    dma("sync", ltri_sb[:], c_ltri, s_ld)
    dma("gpsimd", lnw_tile[:], aa_lnw_t, s_gp)

    chain("vector", s_ve, 3, lambda e, S, W: (
        S(e.memset(onesr[:], 1.0)),
        S(e.memset(onesc[:], 1.0)),
        S(e.memset(carry[:], 0.0))))

    wait("scalar", s_ld)
    chain("scalar", s_ac, 1, lambda e, S, W: (
        S(e.activation(out=ld_tile[:], in_=ld_tile[:], func=AF.Sigmoid)),))
    wait("vector", s_ac)
    chain("vector", s_ve, 1, lambda e, S, W: (
        S(e.tensor_scalar_max(ld_tile[:], ld_tile[:], 1e-7)),))
    wait("scalar", s_ve)
    chain("scalar", s_ac, 1, lambda e, S, W: (
        S(e.activation(out=ld_tile[:], in_=ld_tile[:], func=AF.Ln)),))
    barrier()

    def quant_weight2(dram_w, slot_ap, scale_tile_ap):
        ve_buf = [None, None]
        for c in range(NKC):
            wsrc = wf32 if c % 2 == 0 else wf32b
            sw = s_w if c % 2 == 0 else s_w2
            if ve_buf[c % 2] is not None:
                wait("sync", s_ve, ve_buf[c % 2])
            dma("sync", wsrc[:], dram_w[c * 128:(c + 1) * 128, :], sw)
            wait("vector", sw)
            wait("vector", s_hb)
            def build_q(e, S, W, c=c, wsrc=wsrc):
                S(e.reduce_sum(scl_cols[:, c:c + 1], wsrc[:], axis=AX.X,
                               apply_absolute_value=True))
                W()
                S(e.tensor_scalar(scl_cols[:, c:c + 1], scl_cols[:, c:c + 1],
                                  1.0 / D, 1e-5, ALU.mult, ALU.max))
                W()
                S(e.reciprocal(stat[:, 64:65], scl_cols[:, c:c + 1]))
                W()
                S(e.tensor_scalar(t_a[:], wsrc[:], stat[:, 64:65], None, ALU.mult))
                W()
                S(e.tensor_copy(q32[:], t_a[:]))
                W()
                S(e.tensor_copy(t_c[:], q32[:]))
                W()
                S(e.tensor_tensor(t_b[:], t_a[:], t_c[:], ALU.subtract))
                W()
                S(e.scalar_tensor_tensor(t_c[:], t_b[:], 0.5, t_c[:],
                                         ALU.is_ge, ALU.add))
                S(e.tensor_scalar(t_a[:], t_b[:], -0.5, None, ALU.is_le))
                W()
                S(e.tensor_tensor(t_a[:], t_c[:], t_a[:], ALU.subtract))
                W()
                S(e.tensor_scalar(t_a[:], t_a[:], 1, -1, ALU.min, ALU.max))
            chain("vector", s_ve, 11, build_q)
            ve_buf[c % 2] = s_ve.n
            wait("gpsimd", s_ve)
            dma("gpsimd", wscr[c * 128:(c + 1) * 128, :], t_a[:], s_hb)
        wait("gpsimd", s_hb)
        wait("sync", s_hb)
        def ftr(e, slot_ap=slot_ap):
            for c in range(NKC):
                e.dma_start(out=slot_ap[:, c * D:(c + 1) * D],
                            in_=wscr[:, c * 128:(c + 1) * 128],
                            transpose=True).then_inc(s_tr.h, 16)
        post("sync", ftr, (s_tr, NKC * 16))
        wait("gpsimd", s_ve)
        wait("gpsimd", s_ld)
        def f_row(e):
            e.dma_start(out=rowscr8[0:1, :].rearrange("o (c p) -> p o c", p=128),
                        in_=scl_cols[:, :].unsqueeze(1),
                        allow_slow_non_contiguous=True).then_inc(s_hb.h, 16)
        post("gpsimd", f_row, (s_hb, 16))
        wait("sync", s_hb)
        wait("sync", s_mm)
        dma("sync", carry[0:1, :], rowscr8, s_ld)
        wait("tensor", s_ld)
        wait("tensor", s_ve)
        def ft2(e):
            e.matmul(pc[:, 0:512], onesr[:], carry[0:1, 0:512],
                     start=True, stop=True)
            e.matmul(pc[:, 512:1024], onesr[:], carry[0:1, 512:1024],
                     start=True, stop=True).then_inc(s_mm.h, 1)
        post("tensor", ft2, (s_mm, 1))
        wait("vector", s_mm)
        chain("vector", s_ve, 1, lambda e, S, W: (
            W(), S(e.tensor_copy(scale_tile_ap[:], pc[:]))))
        barrier()

    quant_weight2(w_aa["Wk"], wk_t, sck_tile)
    quant_weight2(w_aa["Wv"], wv_t, scv_tile)
    chain("vector", s_ve, 1, lambda e, S, W: (
        W(), S(e.tensor_tensor(scv_tile[:], scv_tile[:], sck_tile[:], ALU.mult))))
    quant_weight2(w_aa["Wr"], wr_t, scr_tile)
    barrier()

    def ln_resident_to_hscr(src_ap, n_tiles):
        chain("vector", s_ve, n_tiles, lambda e, S, W: [
            S(e.reduce_sum(stat[:, 2 * t:2 * t + 1],
                           src_ap[:, t * D:(t + 1) * D], axis=AX.X))
            for t in range(n_tiles)])

        def build_sq(e, S, W):
            for t in range(n_tiles):
                W()
                S(e.activation(out=t_a[:], in_=src_ap[:, t * D:(t + 1) * D],
                               func=AF.Square, accum_out=stat[:, 2 * t + 1:2 * t + 2]))
        chain("scalar", s_ac, n_tiles, build_sq)
        wait("vector", s_ac)

        def build_fv(e, S, W):
            W()
            for t in range(n_tiles):
                S(e.tensor_scalar_mul(stat[:, 2 * t:2 * t + 1],
                                      stat[:, 2 * t:2 * t + 1], 1.0 / D))
                S(e.tensor_scalar_mul(stat[:, 2 * t + 1:2 * t + 2],
                                      stat[:, 2 * t + 1:2 * t + 2], 1.0 / D))
            W()
            for t in range(n_tiles):
                S(e.tensor_tensor(stat[:, 32 + t:33 + t], stat[:, 2 * t:2 * t + 1],
                                  stat[:, 2 * t:2 * t + 1], ALU.mult))
            W()
            for t in range(n_tiles):
                S(e.tensor_tensor(stat[:, 2 * t + 1:2 * t + 2],
                                  stat[:, 2 * t + 1:2 * t + 2],
                                  stat[:, 32 + t:33 + t], ALU.subtract))
            W()
            for t in range(n_tiles):
                S(e.tensor_scalar_add(stat[:, 2 * t + 1:2 * t + 2],
                                      stat[:, 2 * t + 1:2 * t + 2], 1e-5))
        chain("vector", s_ve, 5 * n_tiles, build_fv)
        wait("scalar", s_ve)
        chain("scalar", s_ac, 1, lambda e, S, W: (
            W(),
            S(e.activation(out=stat[:, 1:2 * n_tiles:2], in_=stat[:, 1:2 * n_tiles:2],
                           func=AF.Sqrt))))
        wait("vector", s_ac)
        chain("vector", s_ve, 1, lambda e, S, W: (
            W(), S(e.reciprocal(stat[:, 1:2 * n_tiles:2], stat[:, 1:2 * n_tiles:2]))))

        for t in range(n_tiles):
            wait("vector", s_hb)
            chain("vector", s_ve, 1, lambda e, S, W, t=t: (
                W(),
                S(e.tensor_scalar(t_a[:], src_ap[:, t * D:(t + 1) * D],
                                  stat[:, 2 * t:2 * t + 1], stat[:, 2 * t + 1:2 * t + 2],
                                  ALU.subtract, ALU.mult))))
            wait("gpsimd", s_ve)
            dma("gpsimd", hscr[t * 128:(t + 1) * 128, :], t_a[:], s_hb)

    def ln_stream_to_hscr(src_dram, n_tiles, keep_tiles=None, dst_t0=0):
        for t in range(n_tiles):
            wait("sync", s_ve)
            wait("sync", s_hb)
            dma("sync", t_b[:], src_dram[t * 128:(t + 1) * 128, :], s_ld)
            wait("vector", s_ld)
            wait("scalar", s_ld)
            chain("vector", s_ve, 2, lambda e, S, W: (
                W(),
                S(e.reduce_sum(stat[:, 0:1], t_b[:], axis=AX.X)),
                W(),
                S(e.tensor_scalar_mul(stat[:, 0:1], stat[:, 0:1], 1.0 / D))))
            wait("scalar", s_hb)
            chain("scalar", s_ac, 1, lambda e, S, W: (
                W(),
                S(e.activation(out=t_a[:], in_=t_b[:], func=AF.Square,
                               accum_out=stat[:, 1:2]))))
            wait("vector", s_ac)
            chain("vector", s_ve, 4, lambda e, S, W: (
                W(),
                S(e.tensor_scalar_mul(stat[:, 1:2], stat[:, 1:2], 1.0 / D)),
                S(e.tensor_tensor(stat[:, 65:66], stat[:, 0:1], stat[:, 0:1], ALU.mult)),
                W(),
                S(e.tensor_tensor(stat[:, 1:2], stat[:, 1:2], stat[:, 65:66],
                                  ALU.subtract)),
                W(),
                S(e.tensor_scalar_add(stat[:, 1:2], stat[:, 1:2], 1e-5))))
            wait("scalar", s_ve)
            chain("scalar", s_ac, 1, lambda e, S, W: (
                W(),
                S(e.activation(out=stat[:, 1:2], in_=stat[:, 1:2], func=AF.Sqrt))))
            wait("vector", s_ac)
            wait("vector", s_hb)
            chain("vector", s_ve, 2, lambda e, S, W: (
                W(),
                S(e.reciprocal(stat[:, 1:2], stat[:, 1:2])),
                W(),
                S(e.tensor_scalar(t_a[:], t_b[:], stat[:, 0:1], stat[:, 1:2],
                                  ALU.subtract, ALU.mult))))
            wait("gpsimd", s_ve)
            dma("gpsimd", hscr[(dst_t0 + t) * 128:(dst_t0 + t + 1) * 128, :],
                t_a[:], s_hb)
            if keep_tiles and t in keep_tiles:
                dma("gpsimd", x1halo_d[keep_tiles[t]:keep_tiles[t] + 128, :], t_b[:], s_hb)

    def transpose_to_hfm(src_dram, nrows):
        def ftr(e, src_dram=src_dram, nrows=nrows):
            for c in range(NKC):
                e.dma_start(out=h_fm[:, c * ROWS:c * ROWS + nrows],
                            in_=src_dram[0:nrows, c * 128:(c + 1) * 128],
                            transpose=True).then_inc(s_tr.h, 16)
        post("sync", ftr, (s_tr, NKC * 16))
        barrier()

    def transpose_halo_to_hfm(src_dram):
        def ftr(e, src_dram=src_dram):
            for c in range(NKC):
                e.dma_start(out=h_fm[:, c * ROWS:c * ROWS + 256],
                            in_=src_dram[ROWS:ROWS + 256, c * 128:(c + 1) * 128],
                            transpose=True).then_inc(s_tr.h, 16)
        post("sync", ftr, (s_tr, NKC * 16))
        barrier()

    def attn_a_tiles():
        for t in range(NT):
            yield t
        barrier()
        transpose_halo_to_hfm(hscr)
        chain("vector", s_ve, 1, lambda e, S, W: (
            W(), S(e.memset(carry[:], 0.0))))
        yield NT
        yield NT + 1

    def attn_a_chain(pos_ap):
        ve_after_st = [None]
        for t in attn_a_tiles():
            full = True
            wait("tensor", s_ve, ve_after_st[0])

            def fmm2(e, t=t, full=full):
                last = None
                for kc in range(NKC):
                    lhsT = h_fm[:, kc * ROWS + TOFF(t): kc * ROWS + TOFF(t) + 128]
                    e.ldweights(lhsT)
                    for ns in range(2):
                        sl = slice(ns * 512, (ns + 1) * 512)
                        last = e.matmul(pk[:, sl], lhsT,
                                        wk_t[:, kc * D + ns * 512: kc * D + (ns + 1) * 512],
                                        start=(kc == 0), stop=(kc == NKC - 1))
                        last = e.matmul(pv[:, sl], lhsT,
                                        wv_t[:, kc * D + ns * 512: kc * D + (ns + 1) * 512],
                                        start=(kc == 0), stop=(kc == NKC - 1))
                        if full:
                            last = e.matmul(pr[:, sl], lhsT,
                                            wr_t[:, kc * D + ns * 512: kc * D + (ns + 1) * 512],
                                            start=(kc == 0), stop=(kc == NKC - 1))
                last.then_inc(s_mm.h, 1)
            post("tensor", fmm2, (s_mm, 1))

            wait("scalar", s_ve, ve_after_st[0])
            def fS(e, t=t, pos_ap=pos_ap):
                e.activation(out=t_c[:], in_=ld_tile[:], func=AF.Exp,
                             scale=pos_ap[:, t:t + 1]).then_inc(s_ac.h, 1)
            post("scalar", fS, (s_ac, 1))

            wait("vector", s_mm)
            wait("vector", s_ac)
            wait("vector", s_hb)

            def build_ev(e, S, W, full=full):
                W()
                S(e.tensor_scalar_max(t_a[:], t_c[:], 1e-10))
                S(e.tensor_copy(t_b[:], pk[:]))
                W()
                S(e.reciprocal(t_a[:], t_a[:]))
                S(e.tensor_tensor(t_b[:], t_b[:], pv[:], ALU.mult))
                W()
                S(e.tensor_tensor(t_b[:], t_b[:], scv_tile[:], ALU.mult))
                W()
                S(e.tensor_tensor(t_b[:], t_b[:], t_a[:], ALU.mult))
                if full:
                    W()
                    S(e.tensor_tensor(t_a[:], pr[:], scr_tile[:], ALU.mult))
            chain("vector", s_ve, 7 if full else 6, build_ev)

            if full:
                wait("scalar", s_ve)
                chain("scalar", s_ac, 1, lambda e, S, W: (
                    S(e.activation(out=t_a[:], in_=t_a[:], func=AF.Sigmoid)),))

            wait("tensor", s_ve)

            def fcum(e):
                for ns in range(2):
                    sl = slice(ns * 512, (ns + 1) * 512)
                    e.matmul(pc[:, sl], ltri_sb[:], t_b[:, sl], start=True, stop=False)
                for ns in range(2):
                    sl = slice(ns * 512, (ns + 1) * 512)
                    e.matmul(pc[:, sl], onesr[:], carry[0:1, sl], start=False, stop=True)
                for ns in range(2):
                    sl = slice(ns * 512, (ns + 1) * 512)
                    i = e.matmul(pk[0:1, sl], onesc[:], t_b[:, sl], start=True, stop=True)
                i.then_inc(s_mm.h, 1)
            post("tensor", fcum, (s_mm, 1))

            wait("vector", s_mm)
            if full:
                wait("vector", s_ac)

            def build_st(e, S, W, full=full):
                W()
                S(e.tensor_tensor(carry[:], carry[:], pk[0:1, 0:D], ALU.add))
                if full:
                    S(e.tensor_tensor(t_b[:], pc[:], t_c[:], ALU.mult))
                    W()
                    S(e.tensor_tensor(t_b[:], t_b[:], t_a[:], ALU.mult))
            chain("vector", s_ve, 3 if full else 1, build_st)
            ve_after_st[0] = s_ve.n

            if full:
                wait("scalar", s_ve)
                wait("scalar", s_hb)
                chain("scalar", s_ac, 1, lambda e, S, W: (
                    W(),
                    S(e.activation(out=t_c[:], in_=t_b[:], func=AF.Square,
                                   accum_out=stat[:, 66:67]))))
                wait("vector", s_ac)
                chain("vector", s_ve, 1, lambda e, S, W: (
                    W(),
                    S(e.tensor_scalar(stat[:, 66:67], stat[:, 66:67], 1.0 / D, 1e-6,
                                      ALU.mult, ALU.add))))
                wait("scalar", s_ve)
                chain("scalar", s_ac, 1, lambda e, S, W: (
                    W(),
                    S(e.activation(out=stat[:, 66:67], in_=stat[:, 66:67], func=AF.Sqrt))))
                wait("vector", s_ac)
                wait("vector", s_hb)
                chain("vector", s_ve, 3, lambda e, S, W: (
                    W(),
                    S(e.reciprocal(stat[:, 66:67], stat[:, 66:67])),
                    W(),
                    S(e.tensor_scalar(t_b[:], t_b[:], stat[:, 66:67], None, ALU.mult)),
                    W(),
                    S(e.tensor_tensor(t_a[:], t_b[:], lnw_tile[:], ALU.mult))))
                wait("gpsimd", s_ve)
                dma("gpsimd", zscr[t * 128:(t + 1) * 128, :], t_a[:], s_hb)

    chain("vector", s_ve, 1, lambda e, S, W: (
        W(), S(e.memset(carry[:], 0.0))))
    ln_resident_to_hscr(xs, NT)
    ln_stream_to_hscr(xhalo_in, 2, keep_tiles={0: 0, 1: 128}, dst_t0=NT)
    wait("sync", s_hb)
    transpose_to_hfm(hscr, ROWS)
    attn_a_chain(pos_sb)
    barrier()

    quant_weight2(w_aa["Wo"], wk_t, sco_tile)
    wait("sync", s_hb)
    transpose_to_hfm(zscr, ROWS)

    def wo_tiles():
        for t in range(NT):
            yield t
        barrier()
        transpose_halo_to_hfm(zscr)
        for t in (NT, NT + 1):
            yield t

    ve_wo = [None, None]
    for wi, t in enumerate(wo_tiles()):
        pWO = pr if wi % 2 == 0 else pc
        if ve_wo[wi % 2] is not None:
            wait("tensor", s_ve, ve_wo[wi % 2])
        def fmm(e, t=t, pWO=pWO):
            last = None
            for kc in range(NKC):
                lhsT = h_fm[:, kc * ROWS + TOFF(t): kc * ROWS + TOFF(t) + 128]
                e.ldweights(lhsT)
                for ns in range(2):
                    sl = slice(ns * 512, (ns + 1) * 512)
                    last = e.matmul(pWO[:, sl], lhsT,
                                    wk_t[:, kc * D + ns * 512: kc * D + (ns + 1) * 512],
                                    start=(kc == 0), stop=(kc == NKC - 1))
            last.then_inc(s_mm.h, 1)
        post("tensor", fmm, (s_mm, 1))
        if t >= NT:
            wait("sync", s_hb)
            wait("sync", s_ve)
            dma("sync", t_b[:], x1halo_d[(t - NT) * 128:(t - NT + 1) * 128, :], s_ld)
            wait("vector", s_ld)
        wait("vector", s_mm)
        def build_fev(e, S, W, t=t, pWO=pWO):
            dst = xs[:, t * D:(t + 1) * D] if t < NT else t_b[:]
            W()
            S(e.tensor_tensor(t_a[:], pWO[:], sco_tile[:], ALU.mult))
            W()
            S(e.tensor_tensor(dst, dst, t_a[:], ALU.add))
        chain("vector", s_ve, 2, build_fev)
        ve_wo[wi % 2] = s_ve.n
        if t >= NT:
            wait("gpsimd", s_ve)
            dma("gpsimd", x1halo_d[(t - NT) * 128:(t - NT + 1) * 128, :], t_b[:], s_hb)
    barrier()


    NTH = NT + 2

    def ffn_round_to(e, S, W):
        W()
        S(e.tensor_copy(q32[:], t_a[:]))
        W()
        S(e.tensor_copy(t_c[:], q32[:]))
        W()
        S(e.tensor_tensor(t_b[:], t_a[:], t_c[:], ALU.subtract))
        W()
        S(e.scalar_tensor_tensor(t_c[:], t_b[:], 0.5, t_c[:], ALU.is_ge, ALU.add))
        S(e.tensor_scalar(t_b[:], t_b[:], -0.5, None, ALU.is_le))
        W()
        S(e.tensor_tensor(t_a[:], t_c[:], t_b[:], ALU.subtract))
    FFN_ROUND_OPS = 6

    def quant_ffn_weight(wT_dram, rows, cols, wq_dram, mean_col):
        nrt = rows // 128
        npc = cols // D
        chain("vector", s_ve, 1, lambda e, S, W: (
            W(), S(e.memset(stat[:, 70:71], 0.0))))
        ve_b1 = [None, None]
        ci = 0
        for rt in range(nrt):
            for pcs in range(npc):
                wsrc = wf32 if ci % 2 == 0 else wf32b
                sw = s_w if ci % 2 == 0 else s_w2
                tcol = 71 if ci % 2 == 0 else 69
                if ve_b1[ci % 2] is not None:
                    wait("sync", s_ve, ve_b1[ci % 2])
                dma("sync", wsrc[:], wT_dram[rt * 128:(rt + 1) * 128,
                                             pcs * D:(pcs + 1) * D], sw)
                wait("vector", sw)
                chain("vector", s_ve, 2, lambda e, S, W, wsrc=wsrc, tcol=tcol: (
                    W(),
                    S(e.reduce_sum(stat[:, tcol:tcol + 1], wsrc[:], axis=AX.X,
                                   apply_absolute_value=True)),
                    W(),
                    S(e.tensor_tensor(stat[:, 70:71], stat[:, 70:71],
                                      stat[:, tcol:tcol + 1], ALU.add))))
                ve_b1[ci % 2] = s_ve.n
                ci += 1
        wait("tensor", s_ve)
        def fsum(e):
            e.matmul(pc[0:1, 0:1], onesc[:], stat[:, 70:71],
                     start=True, stop=True).then_inc(s_mm.h, 1)
        post("tensor", fsum, (s_mm, 1))
        wait("vector", s_mm)
        chain("vector", s_ve, 2, lambda e, S, W: (
            W(),
            S(e.tensor_scalar(stat[0:1, 70:71], pc[0:1, 0:1], 1.0 / (rows * cols),
                              1e-5, ALU.mult, ALU.max)),
            W(),
            S(e.reciprocal(stat[0:1, 69:70], stat[0:1, 70:71]))))
        wait("tensor", s_ve)
        def fbc(e, mean_col=mean_col):
            e.matmul(pc[0:1, 4:5], onesc[0:1, 0:1], stat[0:1, 70:71],
                     start=True, stop=True)
            e.matmul(pc[:, 2:3], onesr[:], stat[0:1, 69:70],
                     start=True, stop=True)
            e.matmul(pc[:, 0:1], onesr[:], stat[0:1, 70:71],
                     start=True, stop=True).then_inc(s_mm.h, 1)
        post("tensor", fbc, (s_mm, 1))
        wait("vector", s_mm)
        chain("vector", s_ve, 2, lambda e, S, W: (
            W(),
            S(e.tensor_copy(stat[:, mean_col:mean_col + 1], pc[:, 0:1])),
            S(e.tensor_copy(stat[:, 68:69], pc[:, 2:3]))))
        wait("sync", s_ve)
        ve_b2 = [None, None]
        ci = 0
        for rt in range(nrt):
            for pcs in range(npc):
                wsrc = wf32 if ci % 2 == 0 else wf32b
                sw = s_w if ci % 2 == 0 else s_w2
                if ve_b2[ci % 2] is not None:
                    wait("sync", s_ve, ve_b2[ci % 2])
                dma("sync", wsrc[:], wT_dram[rt * 128:(rt + 1) * 128,
                                             pcs * D:(pcs + 1) * D], sw)
                wait("vector", sw)
                wait("vector", s_hb)
                chain("vector", s_ve, 4, lambda e, S, W, wsrc=wsrc: (
                    W(),
                    S(e.tensor_scalar(t_a[:], wsrc[:], stat[:, 68:69], None, ALU.mult)),
                    W(),
                    S(e.scalar_tensor_tensor(t_b[:], t_a[:], 0.5, t_a[:],
                                             ALU.is_ge, ALU.bypass)),
                    S(e.tensor_scalar(t_c[:], t_a[:], -0.5, None, ALU.is_le)),
                    W(),
                    S(e.tensor_tensor(t_a[:], t_b[:], t_c[:], ALU.subtract))))
                ve_b2[ci % 2] = s_ve.n
                ci += 1
                wait("gpsimd", s_ve)
                dma("gpsimd", wq_dram[rt * 128:(rt + 1) * 128,
                                      pcs * D:(pcs + 1) * D], t_a[:], s_hb)
        barrier()

    def load_ffn_tiles():
        dma("gpsimd", g1_tile[:], ffn_g1_t, s_gp)
        dma("gpsimd", b1ln_tile[:], ffn_b1ln_t, s_gp)
        dma("gpsimd", bias2_tile[:], ffn_bias2_t, s_gp)
        dma("gpsimd", g2_tile[:], ffn_g2_t, s_gp)
        dma("gpsimd", b2ln_tile[:], ffn_b2ln_t, s_gp)
        dma("gpsimd", bias1_tile[:], ffn_bias1_t, s_gp)
        barrier()

    def emit_ffn(first_call, include_halo=True):
        NTH_e = NTH if include_halo else NT
        if first_call:
            quant_ffn_weight(ffn_w1T, D, HF, w1qscr, 60)
            quant_ffn_weight(ffn_w2T, HF, D, w2qscr, 61)
        load_ffn_tiles()
        dma("sync", w1q_t.rearrange("p (c n) -> p c n", n=HF),
            w1qscr.rearrange("(c p) n -> p c n", p=128), s_w)
        barrier()

        for t in range(NTH_e):
            if t >= NT:
                wait("sync", s_ve)
                wait("sync", s_hb)
                dma("sync", t_b[:], x1halo_d[(t - NT) * 128:(t - NT + 1) * 128, :], s_ld)
                wait("vector", s_ld)
                wait("scalar", s_ld)
                src = t_b[:]
            else:
                src = xs[:, t * D:(t + 1) * D]
            chain("vector", s_ve, 1, lambda e, S, W, src=src: (
                W(), S(e.reduce_sum(stat[:, 0:1], src, axis=AX.X))))
            wait("scalar", s_ve)
            wait("scalar", s_hb)
            chain("scalar", s_ac, 1, lambda e, S, W, src=src: (
                W(),
                S(e.activation(out=t_a[:], in_=src, func=AF.Square,
                               accum_out=stat[:, 1:2]))))
            wait("vector", s_ac)
            chain("vector", s_ve, 7, lambda e, S, W: (
                W(),
                S(e.tensor_scalar_mul(stat[:, 0:1], stat[:, 0:1], 1.0 / D)),
                S(e.tensor_scalar_mul(stat[:, 1:2], stat[:, 1:2], 1.0 / D)),
                W(),
                S(e.tensor_tensor(stat[:, 2:3], stat[:, 0:1], stat[:, 0:1], ALU.mult)),
                W(),
                S(e.tensor_tensor(stat[:, 1:2], stat[:, 1:2], stat[:, 2:3], ALU.subtract)),
                W(),
                S(e.tensor_scalar_add(stat[:, 2:3], stat[:, 1:2], 1e-5)),
                W(),
                S(e.tensor_scalar_add(stat[:, 3:4], stat[:, 1:2], 0.0)),
                S(e.tensor_scalar_add(stat[:, 4:5], stat[:, 2:3], 0.0))))
            wait("scalar", s_ve)
            chain("scalar", s_ac, 1, lambda e, S, W: (
                W(),
                S(e.activation(out=stat[:, 2:3], in_=stat[:, 2:3], func=AF.Sqrt))))
            wait("vector", s_ac)
            chain("vector", s_ve, 4, lambda e, S, W: (
                W(),
                S(e.reciprocal(stat[:, 2:3], stat[:, 2:3])),
                W(),
                S(e.tensor_tensor(stat[:, 5:6], stat[:, 2:3], stat[:, 2:3], ALU.mult)),
                W(),
                S(e.tensor_tensor(stat[:, 5:6], stat[:, 5:6], stat[:, 3:4], ALU.mult)),
                W(),
                S(e.tensor_scalar_add(stat[:, 5:6], stat[:, 5:6], 1e-5))))
            wait("scalar", s_ve)
            chain("scalar", s_ac, 1, lambda e, S, W: (
                W(),
                S(e.activation(out=stat[:, 5:6], in_=stat[:, 5:6], func=AF.Sqrt))))
            wait("vector", s_ac)
            chain("vector", s_ve, 2, lambda e, S, W: (
                W(),
                S(e.reciprocal(stat[:, 5:6], stat[:, 5:6])),
                W(),
                S(e.tensor_tensor(stat[:, 6:7], stat[:, 2:3], stat[:, 5:6], ALU.mult))))
            wait("vector", s_hb)
            chain("vector", s_ve, 2, lambda e, S, W, src=src: (
                W(),
                S(e.scalar_tensor_tensor(t_a[:], src, stat[:, 0:1], g1_tile[:],
                                         ALU.subtract, ALU.mult)),
                W(),
                S(e.scalar_tensor_tensor(t_a[:], t_a[:], stat[:, 6:7], b1ln_tile[:],
                                         ALU.mult, ALU.add))))
            chain("vector", s_ve, 4, lambda e, S, W, t=t: (
                W(),
                S(e.reduce_max(stat[:, 7:8], t_a[:], axis=AX.X,
                               apply_absolute_value=True)),
                W(),
                S(e.tensor_scalar_max(stat[:, 7:8], stat[:, 7:8], 1e-5)),
                W(),
                S(e.reciprocal(stat[:, 8:9], stat[:, 7:8])),
                W(),
                S(e.tensor_scalar_mul(stat[:, 8:9], stat[:, 8:9], 127.0))))
            chain("vector", s_ve, 3, lambda e, S, W, t=t: (
                W(),
                S(e.tensor_tensor(stat[:, 36 + t:37 + t], stat[:, 7:8],
                                  stat[:, 60:61], ALU.mult)),
                W(),
                S(e.tensor_scalar_mul(stat[:, 36 + t:37 + t],
                                      stat[:, 36 + t:37 + t], 1.0 / 127.0)),
                W(),
                S(e.tensor_scalar(t_a[:], t_a[:], stat[:, 8:9], None, ALU.mult))))
            chain("vector", s_ve, FFN_ROUND_OPS, lambda e, S, W: (
                ffn_round_to(e, S, W)))
            wait("gpsimd", s_ve)
            dma("gpsimd", hscr[t * 128:(t + 1) * 128, :], t_a[:], s_hb)
        wait("sync", s_hb)
        transpose_to_hfm(hscr, ROWS)


        def a2_tiles():
            for t in range(NT):
                yield t
            if include_halo:
                barrier()
                transpose_halo_to_hfm(hscr)
                for t in (NT, NT + 1):
                    yield t

        prev_cnt = [None, None, None]
        for t in a2_tiles():
            if prev_cnt[0] is not None:
                wait("tensor", s_ve, prev_cnt[0])
                wait("tensor", s_ac, prev_cnt[2])
            wait("vector", s_hb)
            wait("scalar", s_hb)

            def fmm_h(e, t=t, half=0):
                pX = pA if half == 0 else pB
                last = None
                for kc in range(NKC):
                    lhsT = h_fm[:, kc * ROWS + TOFF(t): kc * ROWS + TOFF(t) + 128]
                    e.ldweights(lhsT)
                    for ns in range(4):
                        w0 = kc * HF + half * 2048 + ns * 512
                        last = e.matmul(pX[:, ns * 512:(ns + 1) * 512], lhsT,
                                        w1q_t[:, w0:w0 + 512],
                                        start=(kc == 0), stop=(kc == NKC - 1))
                last.then_inc(s_mm.h, 1)
            post("tensor", lambda e, t=t: fmm_h(e, t, 0), (s_mm, 1))
            mm_a = s_mm.n
            if prev_cnt[0] is not None:
                wait("tensor", s_ve, prev_cnt[1])
            post("tensor", lambda e, t=t: fmm_h(e, t, 1), (s_mm, 1))

            for half, pX in ((0, pA), (1, pB)):
                wait("vector", s_mm, mm_a + half)
                chain("vector", s_ve, 1, lambda e, S, W, pX=pX, half=half, t=t: (
                    W(),
                    S(e.scalar_tensor_tensor(
                        pX[:], pX[:], stat[:, 36 + t:37 + t],
                        bias1_tile[:, half * 2048:(half + 1) * 2048],
                        ALU.mult, ALU.add))))
                wait("scalar", s_ve)
                chain("scalar", s_ac, 3, lambda e, S, W, pX=pX, half=half: (
                    W(),
                    S(e.activation(out=pX[:], in_=pX[:], func=AF.Gelu,
                                   accum_out=stat[:, 10 + half:11 + half])),
                    W(),
                    S(e.activation(out=t_a[:], in_=pX[:, 0:1024], func=AF.Square,
                                   accum_out=stat[:, 14 + 2 * half:15 + 2 * half])),
                    W(),
                    S(e.activation(out=t_a[:], in_=pX[:, 1024:2048], func=AF.Square,
                                   accum_out=stat[:, 15 + 2 * half:16 + 2 * half]))))
            wait("vector", s_ac)
            chain("vector", s_ve, 7, lambda e, S, W: (
                W(),
                S(e.tensor_tensor(stat[:, 10:11], stat[:, 10:11], stat[:, 11:12], ALU.add)),
                S(e.tensor_tensor(stat[:, 14:15], stat[:, 14:15], stat[:, 15:16], ALU.add)),
                S(e.tensor_tensor(stat[:, 16:17], stat[:, 16:17], stat[:, 17:18], ALU.add)),
                W(),
                S(e.tensor_scalar_mul(stat[:, 10:11], stat[:, 10:11], 1.0 / HF)),
                S(e.tensor_tensor(stat[:, 14:15], stat[:, 14:15], stat[:, 16:17], ALU.add)),
                W(),
                S(e.tensor_scalar_mul(stat[:, 14:15], stat[:, 14:15], 1.0 / HF)),
                W(),
                S(e.tensor_tensor(stat[:, 12:13], stat[:, 10:11], stat[:, 10:11], ALU.mult))))
            chain("vector", s_ve, 2, lambda e, S, W: (
                W(),
                S(e.tensor_tensor(stat[:, 14:15], stat[:, 14:15], stat[:, 12:13], ALU.subtract)),
                W(),
                S(e.tensor_scalar_add(stat[:, 14:15], stat[:, 14:15], 1e-5))))
            wait("scalar", s_ve)
            chain("scalar", s_ac, 1, lambda e, S, W: (
                W(),
                S(e.activation(out=stat[:, 14:15], in_=stat[:, 14:15], func=AF.Sqrt))))
            wait("vector", s_ac)
            chain("vector", s_ve, 1, lambda e, S, W: (
                W(), S(e.reciprocal(stat[:, 14:15], stat[:, 14:15]))))
            for half, pX in ((0, pA), (1, pB)):
                chain("vector", s_ve, 3, lambda e, S, W, pX=pX, half=half: (
                    W(),
                    S(e.scalar_tensor_tensor(pX[:], pX[:], stat[:, 10:11],
                                             g2_tile[:, half * 2048:(half + 1) * 2048],
                                             ALU.subtract, ALU.mult)),
                    W(),
                    S(e.scalar_tensor_tensor(pX[:], pX[:], stat[:, 14:15],
                                             b2ln_tile[:, half * 2048:(half + 1) * 2048],
                                             ALU.mult, ALU.add)),
                    W(),
                    S(e.reduce_max(stat[:, 20 + half:21 + half], pX[:], axis=AX.X,
                                   apply_absolute_value=True))))
            chain("vector", s_ve, 4, lambda e, S, W: (
                W(),
                S(e.tensor_tensor(stat[:, 20:21], stat[:, 20:21], stat[:, 21:22], ALU.max)),
                W(),
                S(e.tensor_scalar_max(stat[:, 20:21], stat[:, 20:21], 1e-5)),
                W(),
                S(e.reciprocal(stat[:, 21:22], stat[:, 20:21])),
                W(),
                S(e.tensor_scalar_mul(stat[:, 21:22], stat[:, 21:22], 127.0))))
            chain("vector", s_ve, 3, lambda e, S, W, t=t: (
                W(),
                S(e.tensor_tensor(stat[:, 54:55], stat[:, 20:21], stat[:, 61:62], ALU.mult)),
                W(),
                S(e.tensor_scalar_mul(stat[:, 54:55], stat[:, 54:55], 1.0 / 127.0)),
                W(),
                S(e.tensor_copy(stat[:, 36 + t:37 + t], stat[:, 54:55]))))
            for j in range(4):
                pj = (pA, pB)[j // 2][:, (j % 2) * 1024:(j % 2) * 1024 + 1024]
                wait("vector", s_hb)
                chain("vector", s_ve, 1 + FFN_ROUND_OPS, lambda e, S, W, pj=pj: (
                    W(),
                    S(e.tensor_scalar(t_a[:], pj[:], stat[:, 21:22], None, ALU.mult)),
                    W(),
                    ffn_round_to(e, S, W)))
                if j == 1:
                    prev_cnt[0] = s_ve.n
                wait("gpsimd", s_ve)
                dma("gpsimd", q2scr[t * 128:(t + 1) * 128, j * D:(j + 1) * D],
                    t_a[:], s_hb)
            prev_cnt[1] = s_ve.n
            prev_cnt[2] = s_ac.n

        barrier()
        dma("sync", w2q_t.rearrange("p (c n) -> p c n", n=D),
            w2qscr.rearrange("(c p) n -> p c n", p=128), s_w)
        barrier()
        ve_fev = [None, None]
        for tg in range(5 if include_halo else 4):
            tiles = list(range(tg * 4, tg * 4 + 4)) if tg < 4 else [NT, NT + 1]
            grows = 512 if tg < 4 else 256
            r0 = tg * 512
            barrier()
            def ftr(e, r0=r0, grows=grows):
                for kc in range(32):
                    e.dma_start(out=h_fm[:, kc * 512: kc * 512 + grows],
                                in_=q2scr[r0:r0 + grows, kc * 128:(kc + 1) * 128],
                                transpose=True).then_inc(s_tr.h, 16)
            post("sync", ftr, (s_tr, 32 * 16))
            barrier()
            for ti, t in enumerate(tiles):
                pPB = pk if ti % 2 == 0 else pv
                if ve_fev[ti % 2] is not None:
                    wait("tensor", s_ve, ve_fev[ti % 2])
                def fmm(e, ti=ti, pB=pPB):
                    last = None
                    for kc in range(32):
                        lhsT = h_fm[:, kc * 512 + ti * 128: kc * 512 + (ti + 1) * 128]
                        e.ldweights(lhsT)
                        for ns in range(2):
                            sl = slice(ns * 512, (ns + 1) * 512)
                            last = e.matmul(pB[:, sl], lhsT,
                                            w2q_t[:, kc * D + ns * 512: kc * D + (ns + 1) * 512],
                                            start=(kc == 0), stop=(kc == 31))
                    last.then_inc(s_mm.h, 1)
                post("tensor", fmm, (s_mm, 1))
                if t >= NT:
                    wait("sync", s_hb)
                    wait("sync", s_ve)
                    dma("sync", t_b[:], x1halo_d[(t - NT) * 128:(t - NT + 1) * 128, :], s_ld)
                    wait("vector", s_ld)
                wait("vector", s_mm)
                def build_fev(e, S, W, t=t, pB=pPB):
                    dst = xs[:, t * D:(t + 1) * D] if t < NT else t_b[:]
                    W()
                    S(e.scalar_tensor_tensor(t_a[:], pB[:], stat[:, 36 + t:37 + t],
                                             bias2_tile[:], ALU.mult, ALU.add))
                    W()
                    S(e.tensor_tensor(dst, dst, t_a[:], ALU.add))
                chain("vector", s_ve, 2, build_fev)
                ve_fev[ti % 2] = s_ve.n
                if t >= NT:
                    wait("gpsimd", s_ve)
                    dma("gpsimd", x1halo_d[(t - NT) * 128:(t - NT + 1) * 128, :], t_b[:], s_hb)
        barrier()

    def emit_attn_b():
        wq_s = AR_W[:, 0:8 * D]
        wk_s = AR_W[:, 8 * D:16 * D]
        wv_s = AR_W[:, 16 * D:24 * D]
        wo_s = AR_W[:, 24 * D:32 * D]
        for slot, drm in ((wq_s, ab_w["Wq"]), (wk_s, ab_w["Wk"]),
                          (wv_s, ab_w["Wv"]), (wo_s, ab_w["Wo"])):
            dma("gpsimd", slot.rearrange("p (c n) -> p c n", n=D),
                drm.rearrange("(c p) n -> p c n", p=128), s_gp)
        dma("gpsimd", bv_tile_b[:], ab_bv_t, s_gp)
        dma("gpsimd", bo_tile_b[:], ab_bo_t, s_gp)
        dma("gpsimd", maskA_b[:], c_maskA, s_gp)
        dma("gpsimd", maskT_b[:], c_maskT, s_gp)
        dma("gpsimd", mb0_b[:], ab_mb0, s_gp)
        dma("gpsimd", mg0_b[:], ab_mg0, s_gp)
        chain("vector", s_ve, 1, lambda e, S, W: (
            W(), S(e.memset(onescb[:], 1.0))))
        ln_resident_to_hscr(xs, NT)
        ln_stream_to_hscr(x1halo_d, 2, dst_t0=NT)
        dma("gpsimd", stat[:, 16:24], ab_bq_c, s_gp)
        dma("gpsimd", stat[:, 24:32], ab_bk_c, s_gp)
        wait("sync", s_hb)
        transpose_halo_to_hfm(hscr)

        def fmm_kqv(e, tok0, want_q):
            last = None
            for b in range(NKC):
                for kc in range(NKC):
                    e.ldweights(wk_s[:, kc * D + b * 128: kc * D + (b + 1) * 128])
                    last = e.matmul(pk[:, b * 128:(b + 1) * 128],
                                    wk_s[:, kc * D + b * 128: kc * D + (b + 1) * 128],
                                    h_fm[:, kc * ROWS + tok0: kc * ROWS + tok0 + 128],
                                    start=(kc == 0), stop=(kc == NKC - 1))
            if want_q:
                for b in range(NKC):
                    for kc in range(NKC):
                        e.ldweights(wq_s[:, kc * D + b * 128: kc * D + (b + 1) * 128])
                        last = e.matmul(pr[:, b * 128:(b + 1) * 128],
                                        wq_s[:, kc * D + b * 128: kc * D + (b + 1) * 128],
                                        h_fm[:, kc * ROWS + tok0: kc * ROWS + tok0 + 128],
                                        start=(kc == 0), stop=(kc == NKC - 1))
            for kc in range(NKC):
                lhsT = h_fm[:, kc * ROWS + tok0: kc * ROWS + tok0 + 128]
                e.ldweights(lhsT)
                for ns in range(2):
                    last = e.matmul(pv[:, ns * 512:(ns + 1) * 512], lhsT,
                                    wv_s[:, kc * D + ns * 512: kc * D + (ns + 1) * 512],
                                    start=(kc == 0), stop=(kc == NKC - 1))
            last.then_inc(s_mm.h, 1)

        def headmaj(dst64, h0):
            return dst64.rearrange("p (h t) -> p h t", t=256)[:, :, 128 * h0:128 * (h0 + 1)]

        def evac_proj(psrc, bcol0, dst64, half2):
            def b_ev(e, S, W):
                W()
                for b in range(NKC):
                    S(e.tensor_scalar(kstg[:, b * 128:(b + 1) * 128],
                                      psrc[:, b * 128:(b + 1) * 128],
                                      stat[:, bcol0 + b:bcol0 + b + 1], None, ALU.add))
                W()
                S(e.tensor_copy(headmaj(dst64[0:64, half2 * 2048:half2 * 2048 + 2048], 0),
                                kstg[0:64, :].rearrange("p (b t) -> p b t", t=128)))
            chain("vector", s_ve, 9, b_ev)
            wait("gpsimd", s_ve)
            dma("gpsimd", headmaj(dst64[0:64, half2 * 2048:half2 * 2048 + 2048], 1),
                kstg[64:128, :].rearrange("p (b t) -> p b t", t=128), s_gp)

        def evac_kqv(half, want_q):
            evac_proj(pk, 24, kt2_sb, half)
            if want_q:
                wait("vector", s_gp)
                evac_proj(pr, 16, qt_sb, 0)
            chain("vector", s_ve, 1, lambda e, S, W, half=half: (
                W(),
                S(e.tensor_tensor(v2_sb[:, half * D:(half + 1) * D], pv[:],
                                  bv_tile_b[:], ALU.add))))
            wait("gpsimd", s_ve)
            dma("gpsimd", vp64_sb[0:64, half * D:(half + 1) * D],
                v2_sb[64:128, half * D:(half + 1) * D], s_gp)

        post("tensor", lambda e: fmm_kqv(e, 128, False), (s_mm, 1))
        wait("vector", s_mm)
        evac_kqv(1, False)
        barrier()
        def fmm_glob(e):
            last = None
            for b in range(NKC):
                for kc in range(NKC):
                    e.ldweights(wk_s[:, kc * D + b * 128: kc * D + (b + 1) * 128])
                    last = e.matmul(pk[:, b:b + 1],
                                    wk_s[:, kc * D + b * 128: kc * D + (b + 1) * 128],
                                    h_fm[:, kc * ROWS: kc * ROWS + 1],
                                    start=(kc == 0), stop=(kc == NKC - 1))
            for kc in range(NKC):
                lhsT = h_fm[:, kc * ROWS: kc * ROWS + 1]
                e.ldweights(lhsT)
                for ns in range(2):
                    last = e.matmul(pv[0:1, ns * 512:(ns + 1) * 512], lhsT,
                                    wv_s[:, kc * D + ns * 512: kc * D + (ns + 1) * 512],
                                    start=(kc == 0), stop=(kc == NKC - 1))
            last.then_inc(s_mm.h, 1)
        post("tensor", fmm_glob, (s_mm, 1))
        wait("vector", s_mm)
        chain("vector", s_ve, 3, lambda e, S, W: (
            W(),
            S(e.tensor_tensor(ktg8[:], pk[:, 0:8], stat[:, 24:32], ALU.add)),
            S(e.tensor_tensor(vg_row, pv[0:1, 0:D], bv_tile_b[0:1, :], ALU.add)),
            W(),
            S(e.tensor_copy(
                ktg_b[0:64, :].rearrange("p (b t) -> p b t", t=2)[:, :, 0:1],
                ktg8[0:64, :].rearrange("p (b t) -> p b t", t=1)))))
        wait("gpsimd", s_ve)
        def f_ktgo(e):
            e.dma_start(
                out=ktg_b[0:64, :].rearrange("p (b t) -> p b t", t=2)[:, :, 1:2],
                in_=ktg8[64:128, :].rearrange("p (b t) -> p b t", t=1),
                allow_slow_non_contiguous=True).then_inc(s_gp.h, 16)
        post("gpsimd", f_ktgo, (s_gp, 16))
        barrier()
        wait("sync", s_hb)
        transpose_to_hfm(hscr, ROWS)

        for t in range(NT):
            cur, prv = t % 2, (t + 1) % 2
            wait("tensor", s_ve)
            post("tensor", lambda e, t=t: fmm_kqv(e, t * 128, True), (s_mm, 1))
            wait("vector", s_mm)
            wait("vector", s_gp)
            evac_kqv(cur, True)
            wait("tensor", s_gp)
            sc_hist = [(None, None), (None, None)]
            av_hist = [None, None]
            for h in range(16):
                hb, hp = h // 2, 64 * (h % 2)
                off = 0 if h % 2 == 0 else 640
                ktA = kt2_sb[0:64, cur * 2048 + h * 128: cur * 2048 + (h + 1) * 128]
                ktB = kt2_sb[0:64, prv * 2048 + h * 128 + 64: prv * 2048 + (h + 1) * 128]
                ktG = ktg_b[0:64, h:h + 1]
                qth = qt_sb[0:64, h * 128:(h + 1) * 128]
                pa_sb, pbg_sb, pg_sb = pa_sb2[h % 2], pbg_sb2[h % 2], pg_sb2[h % 2]
                hac, hve = sc_hist[h % 2]
                if hac is not None:
                    wait("tensor", s_ac, hac)
                    wait("tensor", s_ve, hve)
                def f_sc(e, ktA=ktA, ktB=ktB, ktG=ktG, qth=qth, off=off):
                    e.ldweights(ktA)
                    e.matmul(pk[:, off:off + 128], ktA, qth, start=True, stop=True)
                    e.ldweights(ktB)
                    e.matmul(pk[0:64, off + 128:off + 256], ktB, qth,
                             start=True, stop=True)
                    e.ldweights(ktG)
                    e.matmul(pk[0:1, off + 256:off + 384], ktG, qth,
                             start=True, stop=True).then_inc(s_mm.h, 1)
                post("tensor", f_sc, (s_mm, 1))
                wait("scalar", s_mm)
                chain("scalar", s_ac, 3, lambda e, S, W, off=off: (
                    S(e.activation(out=pk[:, off:off + 128], in_=pk[:, off:off + 128],
                                   func=AF.Exp, scale=0.125)),
                    S(e.activation(out=pk[0:64, off + 128:off + 256],
                                   in_=pk[0:64, off + 128:off + 256],
                                   func=AF.Exp, scale=0.125)),
                    S(e.activation(out=pk[0:1, off + 256:off + 384],
                                   in_=pk[0:1, off + 256:off + 384],
                                   func=AF.Exp, scale=0.125))))
                ac_after = s_ac.n
                wait("vector", s_ac)
                if h >= 2:
                    wait("vector", s_mm, av_hist[h % 2])
                def b_mask(e, S, W, t=t, off=off,
                           pa_sb=pa_sb, pbg_sb=pbg_sb, pg_sb=pg_sb):
                    W()
                    S(e.tensor_tensor(pa_sb[:], pk[:, off:off + 128],
                                      maskA_b[:], ALU.mult))
                    S(e.tensor_tensor(pbg_sb[:], pk[0:64, off + 128:off + 256],
                                      (mb0_b if t == 0 else maskT_b)[:], ALU.mult))
                    if t == 0:
                        S(e.tensor_tensor(pg_sb, pk[0:1, off + 256:off + 384],
                                          mg0_b, ALU.mult))
                    else:
                        S(e.tensor_copy(pg_sb, pk[0:1, off + 256:off + 384]))
                chain("vector", s_ve, 3, b_mask)
                sc_hist[h % 2] = (ac_after, s_ve.n)
                wait("tensor", s_ve)
                def f_av(e, h=h, hb=hb, cur=cur, prv=prv,
                         pa_sb=pa_sb, pbg_sb=pbg_sb, pg_sb=pg_sb):
                    oc = slice(64 * h, 64 * h + 64)
                    e.ldweights(pa_sb[:])
                    e.matmul(pc[:, oc], pa_sb[:],
                             v2_sb[:, cur * D + 64 * h: cur * D + 64 * h + 64],
                             start=True, stop=False)
                    e.matmul(pk[:, 512 + h:513 + h], pa_sb[:], onescb[:],
                             start=True, stop=False)
                    e.ldweights(pbg_sb[:])
                    e.matmul(pc[:, oc], pbg_sb[:],
                             vp64_sb[0:64, prv * D + 64 * h: prv * D + 64 * h + 64],
                             start=False, stop=False)
                    e.matmul(pk[:, 512 + h:513 + h], pbg_sb[:], onescb[0:64, 0:1],
                             start=False, stop=False)
                    e.ldweights(pg_sb)
                    e.matmul(pc[:, oc], pg_sb, vg_row[0:1, 64 * h:64 * h + 64],
                             start=False, stop=True)
                    e.matmul(pk[:, 512 + h:513 + h], pg_sb, onescb[0:1, 0:1],
                             start=False, stop=True).then_inc(s_mm.h, 1)
                post("tensor", f_av, (s_mm, 1))
                av_hist[h % 2] = s_mm.n
            wait("vector", s_mm)
            wait("vector", s_hb)
            def b_norm(e, S, W):
                W()
                S(e.tensor_copy(stat[:, 32:48], pk[:, 512:528]))
                W()
                S(e.reciprocal(stat[:, 32:48], stat[:, 32:48]))
                W()
                for h in range(16):
                    S(e.tensor_scalar(t_a[:, 64 * h:64 * h + 64],
                                      pc[:, 64 * h:64 * h + 64],
                                      stat[:, 32 + h:33 + h], None, ALU.mult))
            chain("vector", s_ve, 18, b_norm)
            wait("gpsimd", s_ve)
            dma("gpsimd", zscr[t * 128:(t + 1) * 128, :], t_a[:], s_hb)
        barrier()

        wait("sync", s_hb)
        transpose_to_hfm(zscr, ROWS)
        ve_wb = [None, None]
        for t in range(NT):
            pWO = pr if t % 2 == 0 else pc
            if ve_wb[t % 2] is not None:
                wait("tensor", s_ve, ve_wb[t % 2])
            def fmm_wo(e, t=t, pWO=pWO):
                last = None
                for kc in range(NKC):
                    lhsT = h_fm[:, kc * ROWS + t * 128: kc * ROWS + (t + 1) * 128]
                    e.ldweights(lhsT)
                    for ns in range(2):
                        last = e.matmul(pWO[:, ns * 512:(ns + 1) * 512], lhsT,
                                        wo_s[:, kc * D + ns * 512: kc * D + (ns + 1) * 512],
                                        start=(kc == 0), stop=(kc == NKC - 1))
                last.then_inc(s_mm.h, 1)
            post("tensor", fmm_wo, (s_mm, 1))
            wait("vector", s_mm)
            chain("vector", s_ve, 2, lambda e, S, W, t=t, pWO=pWO: (
                W(),
                S(e.tensor_tensor(t_a[:], pWO[:], bo_tile_b[:], ALU.add)),
                W(),
                S(e.tensor_tensor(xs[:, t * D:(t + 1) * D], xs[:, t * D:(t + 1) * D],
                                  t_a[:], ALU.add))))
            ve_wb[t % 2] = s_ve.n
        barrier()

    if upto >= 2:
        emit_ffn(first_call=True)
    if upto >= 3:
        emit_attn_b()
    if upto >= 4:
        emit_ffn(first_call=False, include_halo=False)

    s_st = p.sem("s_st")
    p.eng_sems["gpsimd"].append(s_st)
    dma("gpsimd", x_out.rearrange("(t p) d -> p t d", p=128),
        xs[:].rearrange("p (t d) -> p t d", d=D), s_st)
    wait("gpsimd", s_st)
    barrier()

    return p.finalize()



def _prep_launch1_inmaps(i):
    x = np.asarray(i["L0_attn_a_x"])
    maps = []
    for c in range(8):
        b, half = c // 2, c % 2
        xs_np = np.ascontiguousarray(x[b, half * 2048:(half + 1) * 2048, :])
        xhalo = np.ascontiguousarray(
            np.concatenate([x[b, 0:128, :], x[b, 1920:2048, :]], axis=0))
        pos = np.zeros((128, NT + 2), np.float32)
        for t in range(NT):
            pos[:, t] = half * 2048 + t * 128 + np.arange(128)
        pos[:, NT] = np.arange(128)
        pos[:, NT + 1] = 1920 + np.arange(128)
        maps.append({
            "x0": xs_np,
            "xhalo": xhalo,
            "poscols": pos,
            "aa_Wr": np.ascontiguousarray(i["L0_attn_a_Wr"]),
            "aa_Wk": np.ascontiguousarray(i["L0_attn_a_Wk"]),
            "aa_Wv": np.ascontiguousarray(i["L0_attn_a_Wv"]),
            "aa_Wo": np.ascontiguousarray(i["L0_attn_a_Wo"]),
            "aa_decay_t": np.broadcast_to(
                np.asarray(i["L0_attn_a_decay"]).reshape(1, D), (128, D)),
            "aa_lnw_t": np.broadcast_to(
                np.asarray(i["L0_attn_a_ln_w"]).reshape(1, D), (128, D)),
            "ffn_w1T": np.ascontiguousarray(np.asarray(i["L0_ffn_w1"]).T),
            "ffn_w2T": np.ascontiguousarray(np.asarray(i["L0_ffn_w2"]).T),
            "ffn_g1_t": np.broadcast_to(
                np.asarray(i["L0_ffn_ln1_g"]).reshape(1, D), (128, D)),
            "ffn_b1ln_t": np.broadcast_to(
                np.asarray(i["L0_ffn_ln1_b"]).reshape(1, D), (128, D)),
            "ffn_g2_t": np.broadcast_to(
                np.asarray(i["L0_ffn_ln2_g"]).reshape(1, HF), (128, HF)),
            "ffn_b2ln_t": np.broadcast_to(
                np.asarray(i["L0_ffn_ln2_b"]).reshape(1, HF), (128, HF)),
            "ffn_bias1_t": np.broadcast_to(
                np.asarray(i["L0_ffn_b1"]).reshape(1, HF), (128, HF)),
            "ffn_bias2_t": np.broadcast_to(
                np.asarray(i["L0_ffn_b2"]).reshape(1, D), (128, D)),
            "ab_WqT": np.ascontiguousarray(np.asarray(i["L0_attn_b_Wq"]).T),
            "ab_WkT": np.ascontiguousarray(np.asarray(i["L0_attn_b_Wk"]).T),
            "ab_WvT": np.ascontiguousarray(np.asarray(i["L0_attn_b_Wv"]).T),
            "ab_WoT": np.ascontiguousarray(np.asarray(i["L0_attn_b_Wo"]).T),
            "ab_bq_c": np.ascontiguousarray(
                np.asarray(i["L0_attn_b_bq"]).reshape(8, 128).T),
            "ab_bk_c": np.ascontiguousarray(
                np.asarray(i["L0_attn_b_bk"]).reshape(8, 128).T),
            "ab_bv_t": np.broadcast_to(
                np.asarray(i["L0_attn_b_bv"]).reshape(1, D), (128, D)),
            "ab_bo_t": np.broadcast_to(
                np.asarray(i["L0_attn_b_bo"]).reshape(1, D), (128, D)),
            "ab_mb0": (np.zeros((64, 128), np.float32)
                       if half == 0 else
                       (np.arange(128)[None, :] <= np.arange(64)[:, None])
                       .astype(np.float32)),
            "ab_mg0": ((np.arange(128)[None, :] >= 65).astype(np.float32)
                       if half == 0 else np.ones((1, 128), np.float32)),
        })
    return maps


def run_launch1(inputs):
    nc = build_launch1()
    maps = _prep_launch1_inmaps(inputs)
    res = run_bass_kernel_spmd(nc, maps, core_ids=list(range(8)))
    out = np.zeros((4, 4096, D), np.float32)
    for c in range(8):
        b, half = c // 2, c % 2
        out[b, half * 2048:(half + 1) * 2048, :] = res.results[c]["xout"]
    return out



def build_launch2():
    p = Prog()
    nc = p.nc
    post, wait, barrier, chain = p.post, p.wait, p.barrier, p.chain

    x_in = nc.dram_tensor("x0", [ROWS, D], F32, kind="ExternalInput").ap()
    x_out = nc.dram_tensor("xout", [ROWS, D], F32, kind="ExternalOutput").ap()
    wq5_d = nc.dram_tensor("wq5", [D, D], F32, kind="ExternalInput").ap()
    wk5_d = nc.dram_tensor("wk5", [D, D], F32, kind="ExternalInput").ap()
    wv5_d = nc.dram_tensor("wv5", [D, D], F32, kind="ExternalInput").ap()
    wb5_d = nc.dram_tensor("wb5", [D, 16], F32, kind="ExternalInput").ap()
    wo5_d = nc.dram_tensor("wo5", [D, D], F32, kind="ExternalInput").ap()
    lng_t = nc.dram_tensor("lng_t", [128, D], F32, kind="ExternalInput").ap()
    lnb_t = nc.dram_tensor("lnb_t", [128, D], F32, kind="ExternalInput").ap()
    cbt_d = nc.dram_tensor("cbt", [D, D], F32, kind="ExternalInput").ap()
    scb_t_d = nc.dram_tensor("scb_t", [128, D], F32, kind="ExternalInput").ap()
    bias_t_d = nc.dram_tensor("bias_t", [128, D], F32, kind="ExternalInput").ap()
    wq7_d = nc.dram_tensor("wq7", [D, D], F32, kind="ExternalInput").ap()
    wk7_d = nc.dram_tensor("wk7", [D, D], F32, kind="ExternalInput").ap()
    wv7_d = nc.dram_tensor("wv7", [D, D], F32, kind="ExternalInput").ap()
    bq7_c = nc.dram_tensor("bq7_c", [128, 8], F32, kind="ExternalInput").ap()
    bk7_c = nc.dram_tensor("bk7_c", [128, 8], F32, kind="ExternalInput").ap()
    bv7_t = nc.dram_tensor("bv7_t", [128, D], F32, kind="ExternalInput").ap()
    kscr7 = nc.dram_tensor("kscr7", [64, 16 * ROWS], BF16, kind="ExternalOutput").ap()
    qscr7 = nc.dram_tensor("qscr7", [64, 16 * ROWS], BF16, kind="ExternalOutput").ap()
    vscr7 = nc.dram_tensor("vscr7", [ROWS, D], BF16, kind="ExternalOutput").ap()

    xs = p.sbuf("xs", [128, NT * D], F32)
    h_fm = p.sbuf("h_fm", [128, NKC * ROWS], BF16)
    AR_W = p.sbuf("AR_W", [128, 32 * D], BF16)
    wq_s = AR_W[:, 0:8 * D]
    wk_s = AR_W[:, 8 * D:16 * D]
    wv_s = AR_W[:, 16 * D:24 * D]
    wo_s = AR_W[:, 24 * D:32 * D]
    cb_s = AR_W[:, 0:8 * D]
    AR_AUX = p.sbuf("AR_AUX", [128, 15 * D], BF16)
    kb = [AR_AUX[:, s * D:(s + 1) * D] for s in range(4)]
    db = [AR_AUX[:, (4 + s) * D:(5 + s) * D] for s in range(4)]
    g_t = AR_AUX[:, 8 * D:9 * D]
    b_t = AR_AUX[:, 9 * D:10 * D]
    scbt = AR_AUX[:, 10 * D:11 * D]
    biast = AR_AUX[:, 11 * D:12 * D]
    wb_s = AR_AUX[:, 12 * D:12 * D + 128]
    stat = p.sbuf("stat", [128, 72], F32)
    t_a = p.sbuf("t_a", [128, D], F32)
    t_b = p.sbuf("t_b", [128, D], F32)
    t_c = p.sbuf("t_c", [128, D], F32)
    t_d = p.sbuf("t_d", [128, D], F32)

    pk = p.psum("pk", [128, 1024])
    pv = p.psum("pv", [128, 1024])
    pr = p.psum("pr", [128, 1024])
    pc = p.psum("pc", [128, 1024])

    hscr = p.dram("hscr", [ROWS, D], BF16)
    zscr = p.dram("zscr", [ROWS, D], BF16)

    s_ld = p.sem("s_ld")
    s_w = p.sem("s_w")
    s_w2 = p.sem("s_w2")
    s_tr = p.sem("s_tr")
    s_mm = p.sem("s_mm")
    s_ve = p.sem("s_ve")
    s_ac = p.sem("s_ac")
    s_gp = p.sem("s_gp")
    s_hb = p.sem("s_hb")
    p.eng_sems["sync"] = [s_ld, s_w, s_w2, s_tr]
    p.eng_sems["vector"] = [s_ve]
    p.eng_sems["scalar"] = [s_ac]
    p.eng_sems["tensor"] = [s_mm]
    p.eng_sems["gpsimd"] = [s_gp, s_hb]

    def dma(eng, out, in_, sem, n=16):
        def f(e, out=out, in_=in_):
            e.dma_start(out=out, in_=in_).then_inc(sem.h, n)
        post(eng, f, (sem, n))

    H3 = lambda ap: ap.rearrange("p (h d) -> p h d", d=64)
    C3 = lambda ap: ap.rearrange("p (h d) -> p h d", d=1)
    B3 = lambda ap: ap.rearrange("p (h d) -> p h d", d=1).broadcast_to([128, 16, 64])

    dma("sync", xs[:].rearrange("p (t d) -> p t d", d=D),
        x_in.rearrange("(t p) d -> p t d", p=128), s_ld)
    for slot, drm in ((wq_s, wq5_d), (wk_s, wk5_d), (wv_s, wv5_d), (wo_s, wo5_d)):
        dma("gpsimd", slot.rearrange("p (c n) -> p c n", n=D),
            drm.rearrange("(c p) n -> p c n", p=128), s_gp)
    dma("gpsimd", wb_s.rearrange("p (c n) -> p c n", n=16),
        wb5_d.rearrange("(c p) n -> p c n", p=128), s_gp)
    dma("gpsimd", g_t[:], lng_t, s_gp)
    dma("gpsimd", b_t[:], lnb_t, s_gp)
    dma("gpsimd", scbt[:], scb_t_d, s_gp)
    dma("gpsimd", biast[:], bias_t_d, s_gp)
    barrier()

    def ln_resident(n_tiles):
        chain("vector", s_ve, n_tiles, lambda e, S, W: [
            S(e.reduce_sum(stat[:, 2 * t:2 * t + 1],
                           xs[:, t * D:(t + 1) * D], axis=AX.X))
            for t in range(n_tiles)])

        def build_sq(e, S, W):
            for t in range(n_tiles):
                W()
                S(e.activation(out=t_a[:], in_=xs[:, t * D:(t + 1) * D],
                               func=AF.Square, accum_out=stat[:, 2 * t + 1:2 * t + 2]))
        chain("scalar", s_ac, n_tiles, build_sq)
        wait("vector", s_ac)

        def build_fv(e, S, W):
            W()
            for t in range(n_tiles):
                S(e.tensor_scalar_mul(stat[:, 2 * t:2 * t + 1],
                                      stat[:, 2 * t:2 * t + 1], 1.0 / D))
                S(e.tensor_scalar_mul(stat[:, 2 * t + 1:2 * t + 2],
                                      stat[:, 2 * t + 1:2 * t + 2], 1.0 / D))
            W()
            for t in range(n_tiles):
                S(e.tensor_tensor(stat[:, 32 + t:33 + t], stat[:, 2 * t:2 * t + 1],
                                  stat[:, 2 * t:2 * t + 1], ALU.mult))
            W()
            for t in range(n_tiles):
                S(e.tensor_tensor(stat[:, 2 * t + 1:2 * t + 2],
                                  stat[:, 2 * t + 1:2 * t + 2],
                                  stat[:, 32 + t:33 + t], ALU.subtract))
            W()
            for t in range(n_tiles):
                S(e.tensor_scalar_add(stat[:, 2 * t + 1:2 * t + 2],
                                      stat[:, 2 * t + 1:2 * t + 2], 1e-5))
        chain("vector", s_ve, 5 * n_tiles, build_fv)
        wait("scalar", s_ve)
        chain("scalar", s_ac, 1, lambda e, S, W: (
            W(),
            S(e.activation(out=stat[:, 1:2 * n_tiles:2], in_=stat[:, 1:2 * n_tiles:2],
                           func=AF.Sqrt))))
        wait("vector", s_ac)
        chain("vector", s_ve, 1, lambda e, S, W: (
            W(), S(e.reciprocal(stat[:, 1:2 * n_tiles:2], stat[:, 1:2 * n_tiles:2]))))
        for t in range(n_tiles):
            wait("vector", s_hb)
            chain("vector", s_ve, 1, lambda e, S, W, t=t: (
                W(),
                S(e.tensor_scalar(t_a[:], xs[:, t * D:(t + 1) * D],
                                  stat[:, 2 * t:2 * t + 1], stat[:, 2 * t + 1:2 * t + 2],
                                  ALU.subtract, ALU.mult))))
            wait("gpsimd", s_ve)
            dma("gpsimd", hscr[t * 128:(t + 1) * 128, :], t_a[:], s_hb)

    def transpose_to_hfm(src_dram):
        def ftr(e, src_dram=src_dram):
            for c in range(NKC):
                e.dma_start(out=h_fm[:, c * ROWS:(c + 1) * ROWS],
                            in_=src_dram[0:ROWS, c * 128:(c + 1) * 128],
                            transpose=True).then_inc(s_tr.h, 16)
        post("sync", ftr, (s_tr, NKC * 16))
        barrier()

    ln_resident(NT)
    wait("sync", s_hb)
    transpose_to_hfm(hscr)

    def elu_norm(psrc, dst, dst_bf):
        chain("vector", s_ve, 1, lambda e, S, W: (
            W(), S(e.tensor_scalar(t_a[:], psrc[:], 0.0, None, ALU.min))))
        wait("scalar", s_ve)
        chain("scalar", s_ac, 1, lambda e, S, W: (
            W(), S(e.activation(out=t_a[:], in_=t_a[:], func=AF.Exp))))
        wait("vector", s_ac)
        def b_n(e, S, W):
            W()
            S(e.tensor_scalar(t_b[:], psrc[:], 0.0, None, ALU.max))
            W()
            S(e.tensor_tensor(t_b[:], t_b[:], t_a[:], ALU.add))
            W()
            S(e.reduce_sum(C3(stat[:, 32:48]), H3(t_b[:]), axis=AX.X))
            W()
            S(e.reciprocal(stat[:, 32:48], stat[:, 32:48]))
            W()
            S(e.tensor_tensor(H3(dst), H3(t_b[:]), B3(stat[:, 32:48]), ALU.mult))
        chain("vector", s_ve, 5, b_n)

    psum_free = [None, None]
    for pt in range(4):
        for st in range(4):
            tile = st * 4 + pt
            if psum_free[0] is not None:
                wait("tensor", s_ve, psum_free[0])
                wait("tensor", s_ac, psum_free[1])
            def fmm(e, tile=tile):
                last = None
                for kc in range(NKC):
                    lhsT = h_fm[:, kc * ROWS + tile * 128: kc * ROWS + (tile + 1) * 128]
                    e.ldweights(lhsT)
                    for ns in range(2):
                        sl = slice(ns * 512, (ns + 1) * 512)
                        wsl = slice(kc * D + ns * 512, kc * D + (ns + 1) * 512)
                        last = e.matmul(pk[:, sl], lhsT, wk_s[:, wsl],
                                        start=(kc == 0), stop=(kc == NKC - 1))
                        last = e.matmul(pr[:, sl], lhsT, wq_s[:, wsl],
                                        start=(kc == 0), stop=(kc == NKC - 1))
                        last = e.matmul(pv[:, sl], lhsT, wv_s[:, wsl],
                                        start=(kc == 0), stop=(kc == NKC - 1))
                    last = e.matmul(pc[:, 0:16], lhsT,
                                    wb_s[:, kc * 16:(kc + 1) * 16],
                                    start=(kc == 0), stop=(kc == NKC - 1))
                last.then_inc(s_mm.h, 1)
            post("tensor", fmm, (s_mm, 1))
            wait("vector", s_mm)
            wait("scalar", s_mm)
            wait("scalar", s_ve)
            chain("scalar", s_ac, 1, lambda e, S, W: (
                S(e.activation(out=stat[:, 48:64], in_=pc[:, 0:16],
                               func=AF.Sigmoid)),))
            wait("vector", s_hb)
            chain("vector", s_ve, 1, lambda e, S, W: (
                W(), S(e.tensor_copy(t_d[:], pv[:]))))
            elu_norm(pk, kb[st], True)
            elu_norm(pr, t_c, False)
            psum_free = [s_ve.n, s_ac.n]
            def b_delta(e, S, W, st=st):
                W()
                for s2 in range(st):
                    W()
                    S(e.tensor_tensor(t_a[:], kb[s2][:], kb[st][:], ALU.mult))
                    W()
                    S(e.reduce_sum(C3(stat[:, 32:48]), H3(t_a[:]), axis=AX.X))
                    W()
                    S(e.tensor_tensor(H3(t_a[:]), H3(db[s2][:]),
                                      B3(stat[:, 32:48]), ALU.mult))
                    W()
                    S(e.tensor_tensor(t_d[:], t_d[:], t_a[:], ALU.subtract))
            chain("vector", s_ve, 4 * st if st else 0, b_delta) if st else None
            wait("vector", s_ac)
            chain("vector", s_ve, 1, lambda e, S, W, st=st: (
                W(),
                S(e.tensor_tensor(H3(db[st][:]), H3(t_d[:]),
                                  B3(stat[:, 48:64]), ALU.mult))))
            def b_out(e, S, W, st=st):
                for s2 in range(st + 1):
                    W()
                    S(e.tensor_tensor(t_a[:], kb[s2][:], t_c[:], ALU.mult))
                    W()
                    S(e.reduce_sum(C3(stat[:, 32:48]), H3(t_a[:]), axis=AX.X))
                    W()
                    S(e.tensor_tensor(H3(t_a[:]), H3(db[s2][:]),
                                      B3(stat[:, 32:48]), ALU.mult))
                    W()
                    if s2 == 0:
                        S(e.tensor_scalar(t_b[:], t_a[:], 0.125, None, ALU.mult))
                    else:
                        S(e.scalar_tensor_tensor(t_b[:], t_a[:], 0.125, t_b[:],
                                                 ALU.mult, ALU.add))
            chain("vector", s_ve, 4 * (st + 1), b_out)
            wait("gpsimd", s_ve)
            dma("gpsimd", zscr[tile * 128:(tile + 1) * 128, :], t_b[:], s_hb)
    barrier()

    wait("sync", s_hb)
    transpose_to_hfm(zscr)
    ve_wo2 = [None, None]
    for t in range(NT):
        pWO = pr if t % 2 == 0 else pc
        if ve_wo2[t % 2] is not None:
            wait("tensor", s_ve, ve_wo2[t % 2])
        def fmm_wo(e, t=t, pWO=pWO):
            last = None
            for kc in range(NKC):
                lhsT = h_fm[:, kc * ROWS + t * 128: kc * ROWS + (t + 1) * 128]
                e.ldweights(lhsT)
                for ns in range(2):
                    last = e.matmul(pWO[:, ns * 512:(ns + 1) * 512], lhsT,
                                    wo_s[:, kc * D + ns * 512: kc * D + (ns + 1) * 512],
                                    start=(kc == 0), stop=(kc == NKC - 1))
            last.then_inc(s_mm.h, 1)
        post("tensor", fmm_wo, (s_mm, 1))
        wait("vector", s_mm)
        chain("vector", s_ve, 3, lambda e, S, W, t=t, pWO=pWO: (
            W(),
            S(e.tensor_scalar(t_b[:], xs[:, t * D:(t + 1) * D],
                              stat[:, 2 * t:2 * t + 1], stat[:, 2 * t + 1:2 * t + 2],
                              ALU.subtract, ALU.mult)),
            W(),
            S(e.tensor_tensor(t_b[:], t_b[:], pWO[:], ALU.add)),
            W(),
            S(e.reduce_sum(stat[:, 64:65], t_b[:], axis=AX.X))))
        wait("scalar", s_ve)
        chain("scalar", s_ac, 1, lambda e, S, W: (
            W(),
            S(e.activation(out=t_a[:], in_=t_b[:], func=AF.Square,
                           accum_out=stat[:, 65:66]))))
        wait("vector", s_ac)
        chain("vector", s_ve, 5, lambda e, S, W: (
            W(),
            S(e.tensor_scalar_mul(stat[:, 64:65], stat[:, 64:65], 1.0 / D)),
            S(e.tensor_scalar_mul(stat[:, 65:66], stat[:, 65:66], 1.0 / D)),
            W(),
            S(e.tensor_tensor(stat[:, 66:67], stat[:, 64:65], stat[:, 64:65], ALU.mult)),
            W(),
            S(e.tensor_tensor(stat[:, 65:66], stat[:, 65:66], stat[:, 66:67],
                              ALU.subtract)),
            W(),
            S(e.tensor_scalar_add(stat[:, 65:66], stat[:, 65:66], 1e-5))))
        wait("scalar", s_ve)
        chain("scalar", s_ac, 1, lambda e, S, W: (
            W(),
            S(e.activation(out=stat[:, 65:66], in_=stat[:, 65:66], func=AF.Sqrt))))
        wait("vector", s_ac)
        chain("vector", s_ve, 4, lambda e, S, W, t=t: (
            W(),
            S(e.reciprocal(stat[:, 65:66], stat[:, 65:66])),
            W(),
            S(e.scalar_tensor_tensor(t_b[:], t_b[:], stat[:, 64:65], g_t[:],
                                     ALU.subtract, ALU.mult)),
            W(),
            S(e.scalar_tensor_tensor(t_b[:], t_b[:], stat[:, 65:66], b_t[:],
                                     ALU.mult, ALU.add)),
            W(),
            S(e.tensor_tensor(xs[:, t * D:(t + 1) * D], xs[:, t * D:(t + 1) * D],
                              t_b[:], ALU.add))))
        ve_wo2[t % 2] = s_ve.n
    barrier()

    dma("gpsimd", cb_s.rearrange("p (c n) -> p c n", n=D),
        cbt_d.rearrange("(c p) n -> p c n", p=128), s_gp)
    ln_resident(NT)
    wait("sync", s_hb)
    transpose_to_hfm(hscr)
    for t in range(NT):
        pB = pk if t % 2 == 0 else pr
        wait("tensor", s_ve)
        def fmm_cb(e, t=t, pB=pB):
            last = None
            for kc in range(NKC):
                lhsT = h_fm[:, kc * ROWS + t * 128: kc * ROWS + (t + 1) * 128]
                e.ldweights(lhsT)
                for ns in range(2):
                    last = e.matmul(pB[:, ns * 512:(ns + 1) * 512], lhsT,
                                    cb_s[:, kc * D + ns * 512: kc * D + (ns + 1) * 512],
                                    start=(kc == 0), stop=(kc == NKC - 1))
            last.then_inc(s_mm.h, 1)
        post("tensor", fmm_cb, (s_mm, 1))
        wait("vector", s_mm)
        chain("vector", s_ve, 3, lambda e, S, W, t=t, pB=pB: (
            W(),
            S(e.tensor_tensor(t_a[:], pB[:], scbt[:], ALU.mult)),
            W(),
            S(e.scalar_tensor_tensor(t_a[:], t_a[:], 1.0 / 127.0, biast[:],
                                     ALU.mult, ALU.add)),
            W(),
            S(e.tensor_tensor(xs[:, t * D:(t + 1) * D], xs[:, t * D:(t + 1) * D],
                              t_a[:], ALU.add))))
    barrier()

    for slot, drm in ((wq_s, wq7_d), (wk_s, wk7_d), (wv_s, wv7_d)):
        dma("gpsimd", slot.rearrange("p (c n) -> p c n", n=D),
            drm.rearrange("(c p) n -> p c n", p=128), s_gp)
    dma("gpsimd", stat[:, 48:56], bq7_c, s_gp)
    dma("gpsimd", stat[:, 56:64], bk7_c, s_gp)
    dma("gpsimd", g_t[:], bv7_t, s_gp)
    ln_resident(NT)
    wait("sync", s_hb)
    transpose_to_hfm(hscr)
    kstg = kb[0]
    qstg = kb[1]
    for t in range(NT):
        wait("tensor", s_ve)
        def fmm7(e, t=t):
            last = None
            for b in range(NKC):
                for kc in range(NKC):
                    e.ldweights(wk_s[:, kc * D + b * 128: kc * D + (b + 1) * 128])
                    last = e.matmul(pk[:, b * 128:(b + 1) * 128],
                                    wk_s[:, kc * D + b * 128: kc * D + (b + 1) * 128],
                                    h_fm[:, kc * ROWS + t * 128: kc * ROWS + (t + 1) * 128],
                                    start=(kc == 0), stop=(kc == NKC - 1))
            for b in range(NKC):
                for kc in range(NKC):
                    e.ldweights(wq_s[:, kc * D + b * 128: kc * D + (b + 1) * 128])
                    last = e.matmul(pr[:, b * 128:(b + 1) * 128],
                                    wq_s[:, kc * D + b * 128: kc * D + (b + 1) * 128],
                                    h_fm[:, kc * ROWS + t * 128: kc * ROWS + (t + 1) * 128],
                                    start=(kc == 0), stop=(kc == NKC - 1))
            for kc in range(NKC):
                lhsT = h_fm[:, kc * ROWS + t * 128: kc * ROWS + (t + 1) * 128]
                e.ldweights(lhsT)
                for ns in range(2):
                    last = e.matmul(pv[:, ns * 512:(ns + 1) * 512], lhsT,
                                    wv_s[:, kc * D + ns * 512: kc * D + (ns + 1) * 512],
                                    start=(kc == 0), stop=(kc == NKC - 1))
            last.then_inc(s_mm.h, 1)
        post("tensor", fmm7, (s_mm, 1))
        wait("vector", s_mm)
        wait("vector", s_hb)
        def b_ev7(e, S, W):
            W()
            for b in range(NKC):
                S(e.tensor_scalar(kstg[:, b * 128:(b + 1) * 128],
                                  pk[:, b * 128:(b + 1) * 128],
                                  stat[:, 56 + b:57 + b], None, ALU.add))
            for b in range(NKC):
                S(e.tensor_scalar(qstg[:, b * 128:(b + 1) * 128],
                                  pr[:, b * 128:(b + 1) * 128],
                                  stat[:, 48 + b:49 + b], None, ALU.add))
            S(e.tensor_tensor(t_a[:], pv[:], g_t[:], ALU.add))
        chain("vector", s_ve, 17, b_ev7)
        wait("gpsimd", s_ve)
        for src, dst in ((kstg, kscr7), (qstg, qscr7)):
            r3 = dst.rearrange("p (b r) -> p b r", r=4096)
            dma("gpsimd", r3[:, :, t * 128:(t + 1) * 128],
                src[0:64, :].rearrange("p (b t) -> p b t", t=128), s_hb)
            dma("gpsimd", r3[:, :, 2048 + t * 128:2048 + (t + 1) * 128],
                src[64:128, :].rearrange("p (b t) -> p b t", t=128), s_hb)
        dma("gpsimd", vscr7[t * 128:(t + 1) * 128, :], t_a[:], s_hb)
    barrier()

    s_st = p.sem("s_st")
    p.eng_sems["gpsimd"].append(s_st)
    dma("gpsimd", x_out.rearrange("(t p) d -> p t d", p=128),
        xs[:].rearrange("p (t d) -> p t d", d=D), s_st)
    wait("gpsimd", s_st)
    barrier()
    return p.finalize()


def _prep_launch2_inmaps(i, x4):
    Wq = np.asarray(i["L1_attn_a_W_qkvb"]).reshape(D, 16, 193)
    wq5 = np.ascontiguousarray(Wq[:, :, 0:64].reshape(D, D))
    wk5 = np.ascontiguousarray(Wq[:, :, 64:128].reshape(D, D))
    wv5 = np.ascontiguousarray(Wq[:, :, 128:192].reshape(D, D))
    wb5 = np.ascontiguousarray(Wq[:, :, 192])
    wo5 = np.ascontiguousarray(np.asarray(i["L1_attn_a_W_o"]))
    lng = np.broadcast_to(np.asarray(i["L1_attn_a_ln_g"]).reshape(1, D), (128, D))
    lnb = np.broadcast_to(np.asarray(i["L1_attn_a_ln_b"]).reshape(1, D), (128, D))
    cbt = np.ascontiguousarray(np.asarray(i["L1_ffn_CB"]).T)
    scb = np.broadcast_to(np.asarray(i["L1_ffn_SCB"]).reshape(1, D), (128, D))
    bia = np.broadcast_to(np.asarray(i["L1_ffn_bias"]).reshape(1, D), (128, D))
    wq7 = np.ascontiguousarray(
        np.asarray(i["L1_attn_b_wq"]).transpose(1, 0, 2).reshape(D, D))
    wk7 = np.ascontiguousarray(
        np.asarray(i["L1_attn_b_wk"]).transpose(1, 0, 2).reshape(D, D))
    wv7 = np.ascontiguousarray(
        np.asarray(i["L1_attn_b_wv"]).transpose(1, 0, 2).reshape(D, D))
    bq7 = np.ascontiguousarray(
        np.asarray(i["L1_attn_b_bq"]).reshape(D).reshape(8, 128).T)
    bk7 = np.ascontiguousarray(
        np.asarray(i["L1_attn_b_bk"]).reshape(D).reshape(8, 128).T)
    bv7 = np.broadcast_to(np.asarray(i["L1_attn_b_bv"]).reshape(1, D), (128, D))
    maps = []
    for c in range(8):
        xl = np.ascontiguousarray(
            x4[:, 512 * c:512 * (c + 1), :].reshape(ROWS, D))
        maps.append({
            "x0": xl, "wq5": wq5, "wk5": wk5, "wv5": wv5, "wb5": wb5,
            "wo5": wo5, "lng_t": lng, "lnb_t": lnb, "cbt": cbt,
            "scb_t": scb, "bias_t": bia,
            "wq7": wq7, "wk7": wk7, "wv7": wv7,
            "bq7_c": bq7, "bk7_c": bk7, "bv7_t": bv7,
        })
    return maps


def run_launch2(i, x4):
    nc = build_launch2()
    maps = _prep_launch2_inmaps(i, x4)
    res = run_bass_kernel_spmd(nc, maps, core_ids=list(range(8)))
    out = np.zeros((4, 4096, D), np.float32)
    scr = []
    for c in range(8):
        out[:, 512 * c:512 * (c + 1), :] = res.results[c]["xout"].reshape(4, 512, D)
        scr.append({k: res.results[c][k] for k in ("kscr7", "qscr7", "vscr7")})
    return out, scr



def build_launch3():
    p = Prog()
    nc = p.nc
    post, wait, barrier, chain = p.post, p.wait, p.barrier, p.chain

    x_in = nc.dram_tensor("x0", [ROWS, D], F32, kind="ExternalInput").ap()
    x_out = nc.dram_tensor("xout", [ROWS, D], F32, kind="ExternalOutput").ap()
    qt7_in = nc.dram_tensor("qt7", [64, 16 * ROWS], BF16, kind="ExternalInput").ap()
    kt7_in = nc.dram_tensor("kt7", [64, 16 * 4096], BF16, kind="ExternalInput").ap()
    v7_in = nc.dram_tensor("v7", [4096, D], BF16, kind="ExternalInput").ap()
    wo7_d = nc.dram_tensor("wo7T", [D, D], F32, kind="ExternalInput").ap()
    bo7_d = nc.dram_tensor("bo7_t", [128, D], F32, kind="ExternalInput").ap()
    cbt_d = nc.dram_tensor("cbt", [D, D], F32, kind="ExternalInput").ap()
    scb_t_d = nc.dram_tensor("scb_t", [128, D], F32, kind="ExternalInput").ap()
    bias_t_d = nc.dram_tensor("bias_t", [128, D], F32, kind="ExternalInput").ap()

    xs = p.sbuf("xs", [128, NT * D], F32)
    h_fm = p.sbuf("h_fm", [128, NKC * ROWS], BF16)
    AR_W = p.sbuf("AR_W", [128, 32 * D], BF16)
    ktb = AR_W[0:64, 0:8192]
    vb = AR_W[:, 8192:12352]
    qtb = AR_W[0:64, 12352:13376]
    ptb = [AR_W[:, 13376:13888], AR_W[:, 13888:14400]]
    aodd = AR_W[0:64, 14400:14912]
    den64 = AR_W[64:65, 15360:17408]
    wo_s = AR_W[:, 0:8 * D]
    cb_s = AR_W[:, 0:8 * D]
    bo7 = AR_W[:, 31744:32768]
    att_fm = h_fm[:, 0:16384]
    scbt = p.sbuf("scbt", [128, D], BF16)
    biast = p.sbuf("biast", [128, D], BF16)
    stat = p.sbuf("stat", [128, 72], F32)
    t_a = p.sbuf("t_a", [128, D], F32)
    t_b = p.sbuf("t_b", [128, D], F32)
    denr = p.sbuf("denr", [1, 1024], F32)
    onesr = p.sbuf("onesr", [1, 128], F32)

    psc = [p.psum("psc0", [128, 512]), p.psum("psc1", [128, 512])]
    pou = [p.psum("pou0", [128, 512]), p.psum("pou1", [128, 512])]
    pwo = p.psum("pwo", [128, 1024])
    pwo2 = p.psum("pwo2", [128, 1024])

    hscr = p.dram("hscr", [ROWS, D], BF16)

    s_ld = p.sem("s_ld")
    s_w = p.sem("s_w")
    s_w2 = p.sem("s_w2")
    s_tr = p.sem("s_tr")
    s_mm = p.sem("s_mm")
    s_ve = p.sem("s_ve")
    s_ac = p.sem("s_ac")
    s_gp = p.sem("s_gp")
    s_hb = p.sem("s_hb")
    p.eng_sems["sync"] = [s_ld, s_w, s_w2, s_tr]
    p.eng_sems["vector"] = [s_ve]
    p.eng_sems["scalar"] = [s_ac]
    p.eng_sems["tensor"] = [s_mm]
    p.eng_sems["gpsimd"] = [s_gp, s_hb]

    def dma(eng, out, in_, sem, n=16):
        def f(e, out=out, in_=in_):
            e.dma_start(out=out, in_=in_).then_inc(sem.h, n)
        post(eng, f, (sem, n))

    dma("sync", xs[:].rearrange("p (t d) -> p t d", d=D),
        x_in.rearrange("(t p) d -> p t d", p=128), s_ld)
    dma("gpsimd", bo7.rearrange("p (c n) -> p c n", n=D), bo7_d.unsqueeze(1), s_gp)
    dma("gpsimd", scbt[:], scb_t_d, s_gp)
    dma("gpsimd", biast[:], bias_t_d, s_gp)
    chain("vector", s_ve, 1, lambda e, S, W: (
        W(), S(e.memset(onesr[:], 1.0))))
    barrier()

    mm_sc_end = [None]
    for hb in range(8):
        wait("sync", s_mm, mm_sc_end[0])
        dma("sync", ktb[:], kt7_in[:, hb * 8192:(hb + 1) * 8192], s_w)
        wait("vector", s_mm)
        chain("vector", s_ve, 1, lambda e, S, W: (
            W(), S(e.memset(vb[:], 1.0))))
        wait("sync", s_ve)
        for h2 in range(2):
            dma("sync", vb.rearrange("p (t c) -> p t c", c=130)[:, :, h2 * 65:h2 * 65 + 64],
                v7_in[:, 64 * (2 * hb + h2):64 * (2 * hb + h2 + 1)]
                .rearrange("(t p) c -> p t c", p=128), s_w2)
        wait("tensor", s_w)
        wait("tensor", s_w2)
        wait("tensor", s_ve)
        for qg in range(4):
            wait("sync", s_mm)
            wait("sync", s_gp)
            dma("sync", qtb.rearrange("p (h t) -> p h t", t=512),
                qt7_in.rearrange("p (h t) -> p h t", t=ROWS)[
                    :, 2 * hb:2 * hb + 2, qg * 512:(qg + 1) * 512], s_ld)
            wait("tensor", s_ld)
            wait("tensor", s_ve)
            for h2 in range(2):
                po = pou[h2]
                exp_cnt = []
                for kt in range(34):
                    if kt < 32:
                        b = kt % 2
                        if kt >= 2:
                            wait("tensor", s_ac, exp_cnt[kt - 2])
                        def f_sc(e, h2=h2, kt=kt, b=b):
                            src = ktb[:, h2 * 4096 + kt * 128: h2 * 4096 + (kt + 1) * 128]
                            e.ldweights(src)
                            e.matmul(psc[b][:], src, qtb[:, h2 * 512:(h2 + 1) * 512],
                                     start=True, stop=True).then_inc(s_mm.h, 1)
                        post("tensor", f_sc, (s_mm, 1))
                        if h2 == 1 and kt == 31 and qg == 3:
                            mm_sc_end[0] = s_mm.n
                    if kt >= 2:
                        ka = kt - 2
                        wait("tensor", s_ac, exp_cnt[ka])
                        def f_av(e, h2=h2, ka=ka, po=po):
                            v_sl = vb[:, ka * 130 + h2 * 65: ka * 130 + (h2 + 1) * 65]
                            e.ldweights(v_sl)
                            e.matmul(po[0:65, :], v_sl, ptb[ka % 2][:],
                                     start=(ka == 0), stop=(ka == 31)).then_inc(s_mm.h, 1)
                        post("tensor", f_av, (s_mm, 1))
                    if kt < 32:
                        wait("scalar", s_mm)
                        chain("scalar", s_ac, 1, lambda e, S, W, b=b: (
                            S(e.activation(out=ptb[b][:], in_=psc[b][:], func=AF.Exp,
                                           scale=0.125)),))
                        exp_cnt.append(s_ac.n)
            qcols = slice(hb * 2048 + qg * 512, hb * 2048 + (qg + 1) * 512)
            wait("vector", s_mm)
            wait("vector", s_gp)
            chain("vector", s_ve, 2, lambda e, S, W: (
                W(),
                S(e.tensor_copy(den64[:].bitcast(F32)[0:1, 0:512], pou[0][64:65, :])),
                S(e.tensor_copy(den64[:].bitcast(F32)[0:1, 512:1024], pou[1][64:65, :]))))
            wait("gpsimd", s_ve)
            dma("gpsimd", denr[:], den64[:].bitcast(F32), s_gp)
            wait("vector", s_gp)
            chain("vector", s_ve, 1, lambda e, S, W: (
                W(), S(e.reciprocal(denr[:], denr[:]))))
            wait("tensor", s_ve)
            def f_bc(e):
                e.matmul(pwo[:, 0:512], onesr[:], denr[0:1, 0:512],
                         start=True, stop=True)
                e.matmul(pwo[:, 512:1024], onesr[:], denr[0:1, 512:1024],
                         start=True, stop=True).then_inc(s_mm.h, 1)
            post("tensor", f_bc, (s_mm, 1))
            wait("vector", s_mm)
            wait("vector", s_hb)
            def b_att(e, S, W, qcols=qcols):
                W()
                S(e.tensor_copy(t_a[:], pwo[:]))
                W()
                S(e.tensor_tensor(att_fm[0:64, qcols], pou[0][0:64, :],
                                  t_a[0:64, 0:512], ALU.mult))
                S(e.tensor_tensor(aodd[:], pou[1][0:64, :],
                                  t_a[0:64, 512:1024], ALU.mult))
            chain("vector", s_ve, 3, b_att)
            wait("gpsimd", s_ve)
            dma("gpsimd", att_fm[64:128, qcols], aodd[:], s_hb)

    dma("gpsimd", wo_s.rearrange("p (c n) -> p c n", n=D),
        wo7_d.rearrange("(c p) n -> p c n", p=128), s_gp)
    barrier()
    ve_w3 = [None, None]
    for t in range(NT):
        pW = pwo if t % 2 == 0 else pwo2
        if ve_w3[t % 2] is not None:
            wait("tensor", s_ve, ve_w3[t % 2])
        def fmm_wo(e, t=t, pW=pW):
            last = None
            for kc in range(NKC):
                lhsT = att_fm[:, kc * ROWS + t * 128: kc * ROWS + (t + 1) * 128]
                e.ldweights(lhsT)
                for ns in range(2):
                    last = e.matmul(pW[:, ns * 512:(ns + 1) * 512], lhsT,
                                    wo_s[:, kc * D + ns * 512: kc * D + (ns + 1) * 512],
                                    start=(kc == 0), stop=(kc == NKC - 1))
            last.then_inc(s_mm.h, 1)
        post("tensor", fmm_wo, (s_mm, 1))
        wait("vector", s_mm)
        chain("vector", s_ve, 2, lambda e, S, W, t=t, pW=pW: (
            W(),
            S(e.tensor_tensor(t_a[:], pW[:], bo7[:], ALU.add)),
            W(),
            S(e.tensor_tensor(xs[:, t * D:(t + 1) * D], xs[:, t * D:(t + 1) * D],
                              t_a[:], ALU.add))))
        ve_w3[t % 2] = s_ve.n
    barrier()

    dma("gpsimd", cb_s.rearrange("p (c n) -> p c n", n=D),
        cbt_d.rearrange("(c p) n -> p c n", p=128), s_gp)

    def ln_resident(n_tiles):
        chain("vector", s_ve, n_tiles, lambda e, S, W: [
            S(e.reduce_sum(stat[:, 2 * t:2 * t + 1],
                           xs[:, t * D:(t + 1) * D], axis=AX.X))
            for t in range(n_tiles)])
        def build_sq(e, S, W):
            for t in range(n_tiles):
                W()
                S(e.activation(out=t_a[:], in_=xs[:, t * D:(t + 1) * D],
                               func=AF.Square, accum_out=stat[:, 2 * t + 1:2 * t + 2]))
        chain("scalar", s_ac, n_tiles, build_sq)
        wait("vector", s_ac)
        def build_fv(e, S, W):
            W()
            for t in range(n_tiles):
                S(e.tensor_scalar_mul(stat[:, 2 * t:2 * t + 1],
                                      stat[:, 2 * t:2 * t + 1], 1.0 / D))
                S(e.tensor_scalar_mul(stat[:, 2 * t + 1:2 * t + 2],
                                      stat[:, 2 * t + 1:2 * t + 2], 1.0 / D))
            W()
            for t in range(n_tiles):
                S(e.tensor_tensor(stat[:, 32 + t:33 + t], stat[:, 2 * t:2 * t + 1],
                                  stat[:, 2 * t:2 * t + 1], ALU.mult))
            W()
            for t in range(n_tiles):
                S(e.tensor_tensor(stat[:, 2 * t + 1:2 * t + 2],
                                  stat[:, 2 * t + 1:2 * t + 2],
                                  stat[:, 32 + t:33 + t], ALU.subtract))
            W()
            for t in range(n_tiles):
                S(e.tensor_scalar_add(stat[:, 2 * t + 1:2 * t + 2],
                                      stat[:, 2 * t + 1:2 * t + 2], 1e-5))
        chain("vector", s_ve, 5 * n_tiles, build_fv)
        wait("scalar", s_ve)
        chain("scalar", s_ac, 1, lambda e, S, W: (
            W(),
            S(e.activation(out=stat[:, 1:2 * n_tiles:2], in_=stat[:, 1:2 * n_tiles:2],
                           func=AF.Sqrt))))
        wait("vector", s_ac)
        chain("vector", s_ve, 1, lambda e, S, W: (
            W(), S(e.reciprocal(stat[:, 1:2 * n_tiles:2], stat[:, 1:2 * n_tiles:2]))))
        for t in range(n_tiles):
            wait("vector", s_hb)
            chain("vector", s_ve, 1, lambda e, S, W, t=t: (
                W(),
                S(e.tensor_scalar(t_a[:], xs[:, t * D:(t + 1) * D],
                                  stat[:, 2 * t:2 * t + 1], stat[:, 2 * t + 1:2 * t + 2],
                                  ALU.subtract, ALU.mult))))
            wait("gpsimd", s_ve)
            dma("gpsimd", hscr[t * 128:(t + 1) * 128, :], t_a[:], s_hb)

    ln_resident(NT)
    wait("sync", s_hb)
    def ftr(e):
        for c in range(NKC):
            e.dma_start(out=h_fm[:, c * ROWS:(c + 1) * ROWS],
                        in_=hscr[0:ROWS, c * 128:(c + 1) * 128],
                        transpose=True).then_inc(s_tr.h, 16)
    post("sync", ftr, (s_tr, NKC * 16))
    barrier()
    ve_c3 = [None, None]
    for t in range(NT):
        pW = pwo if t % 2 == 0 else pwo2
        if ve_c3[t % 2] is not None:
            wait("tensor", s_ve, ve_c3[t % 2])
        def fmm_cb(e, t=t, pW=pW):
            last = None
            for kc in range(NKC):
                lhsT = h_fm[:, kc * ROWS + t * 128: kc * ROWS + (t + 1) * 128]
                e.ldweights(lhsT)
                for ns in range(2):
                    last = e.matmul(pW[:, ns * 512:(ns + 1) * 512], lhsT,
                                    cb_s[:, kc * D + ns * 512: kc * D + (ns + 1) * 512],
                                    start=(kc == 0), stop=(kc == NKC - 1))
            last.then_inc(s_mm.h, 1)
        post("tensor", fmm_cb, (s_mm, 1))
        wait("vector", s_mm)
        chain("vector", s_ve, 3, lambda e, S, W, t=t, pW=pW: (
            W(),
            S(e.tensor_tensor(t_a[:], pW[:], scbt[:], ALU.mult)),
            W(),
            S(e.scalar_tensor_tensor(t_a[:], t_a[:], 1.0 / 127.0, biast[:],
                                     ALU.mult, ALU.add)),
            W(),
            S(e.tensor_tensor(xs[:, t * D:(t + 1) * D], xs[:, t * D:(t + 1) * D],
                              t_a[:], ALU.add))))
        ve_c3[t % 2] = s_ve.n
    barrier()

    s_st = p.sem("s_st")
    p.eng_sems["gpsimd"].append(s_st)
    dma("gpsimd", x_out.rearrange("(t p) d -> p t d", p=128),
        xs[:].rearrange("p (t d) -> p t d", d=D), s_st)
    wait("gpsimd", s_st)
    barrier()
    return p.finalize()


def _prep_launch3_inmaps(i, x6, scr):
    wo7 = np.ascontiguousarray(np.asarray(i["L1_attn_b_wo"]).T)
    bo7 = np.broadcast_to(np.asarray(i["L1_attn_b_bo"]).reshape(1, D), (128, D))
    cbt = np.ascontiguousarray(np.asarray(i["L1_ffn_CB"]).T)
    scb = np.broadcast_to(np.asarray(i["L1_ffn_SCB"]).reshape(1, D), (128, D))
    bia = np.broadcast_to(np.asarray(i["L1_ffn_bias"]).reshape(1, D), (128, D))
    maps = []
    for c8 in range(8):
        b, half = c8 // 2, c8 % 2
        kt = np.zeros((64, 16 * 4096), ml_dtypes.bfloat16)
        v7 = np.zeros((4096, D), ml_dtypes.bfloat16)
        for c in range(8):
            ks = scr[c]["kscr7"]
            vs = scr[c]["vscr7"]
            rows = slice(b * 512, (b + 1) * 512)
            pos = slice(512 * c, 512 * (c + 1))
            for h in range(16):
                src = ks[:, (h // 2) * 4096 + (h % 2) * 2048:][:, rows]
                kt[:, h * 4096:][:, pos] = src
            v7[pos, :] = vs[rows, :]
        qt = np.zeros((64, 16 * ROWS), ml_dtypes.bfloat16)
        for c in range(half * 4, half * 4 + 4):
            qs = scr[c]["qscr7"]
            rows = slice(b * 512, (b + 1) * 512)
            qpos = slice(512 * c - half * 2048, 512 * (c + 1) - half * 2048)
            for h in range(16):
                qt[:, h * ROWS:][:, qpos] =                     qs[:, (h // 2) * 4096 + (h % 2) * 2048:][:, rows]
        maps.append({
            "x0": np.ascontiguousarray(x6[b, half * 2048:(half + 1) * 2048, :]),
            "qt7": qt, "kt7": kt, "v7": v7,
            "wo7T": wo7, "bo7_t": bo7, "cbt": cbt, "scb_t": scb, "bias_t": bia,
        })
    return maps


def run_launch3(i, x6, scr):
    nc = build_launch3()
    maps = _prep_launch3_inmaps(i, x6, scr)
    res = run_bass_kernel_spmd(nc, maps, core_ids=list(range(8)))
    out = np.zeros((4, 4096, D), np.float32)
    for c in range(8):
        b, half = c // 2, c % 2
        out[b, half * 2048:(half + 1) * 2048, :] = res.results[c]["xout"]
    return out


_nc_cache = {}


def _get_nc(name, builder):
    if name not in _nc_cache:
        _nc_cache[name] = builder()
    return _nc_cache[name]


def kernel(**inputs):
    i = {k: np.asarray(v) for k, v in inputs.items()}
    nc1 = _get_nc("l1", build_launch1)
    maps1 = _prep_launch1_inmaps(i)
    res1 = run_bass_kernel_spmd(nc1, maps1, core_ids=list(range(8)))
    x4 = np.zeros((4, 4096, D), np.float32)
    for c in range(8):
        b, half = c // 2, c % 2
        x4[b, half * 2048:(half + 1) * 2048, :] = res1.results[c]["xout"]
    nc2 = _get_nc("l2", build_launch2)
    maps2 = _prep_launch2_inmaps(i, x4)
    res2 = run_bass_kernel_spmd(nc2, maps2, core_ids=list(range(8)))
    x6 = np.zeros((4, 4096, D), np.float32)
    scr = []
    for c in range(8):
        x6[:, 512 * c:512 * (c + 1), :] = res2.results[c]["xout"].reshape(4, 512, D)
        scr.append({k: res2.results[c][k] for k in ("kscr7", "qscr7", "vscr7")})
    nc3 = _get_nc("l3", build_launch3)
    maps3 = _prep_launch3_inmaps(i, x6, scr)
    res3 = run_bass_kernel_spmd(nc3, maps3, core_ids=list(range(8)))
    x8 = np.zeros((4, 4096, D), np.float32)
    for c in range(8):
        b, half = c // 2, c % 2
        x8[b, half * 2048:(half + 1) * 2048, :] = res3.results[c]["xout"]
    return x8
